# version 01 (final)
# baseline (speedup 1.0000x reference)
import math
from collections import defaultdict
from contextlib import ExitStack

import numpy as np

import concourse.bass as bass
import concourse.mybir as mybir
from concourse.bass_utils import run_bass_kernel_spmd

F32 = mybir.dt.float32
BF16 = mybir.dt.bfloat16
AF = mybir.ActivationFunctionType
ALU = mybir.AluOpType
AX = mybir.AxisListType

ENGINES = ("pe", "act", "dve", "pool", "sp")
DMA_POOLS = {"dsp": 4, "dpool": 2}
DMA_SEMS = tuple(f"{q}{i}" for q, n in DMA_POOLS.items() for i in range(n))


class Buf:
    __slots__ = ("w", "r", "name")

    def __init__(self, name=""):
        self.w = None
        self.r = []
        self.name = name


class Plan:

    def __init__(self, nc):
        self.nc = nc
        self.sems = {s: nc.alloc_semaphore(f"sem_{s}") for s in ENGINES + DMA_SEMS}
        self.count = defaultdict(int)
        self.waited = {e: defaultdict(int) for e in ENGINES}
        self.ops = {e: [] for e in ENGINES}
        self.dma_rot = defaultdict(int)

    def _waits(self, eng, reads, writes):
        need = {}

        def add(tok):
            if tok is None:
                return
            s, v = tok
            if need.get(s, 0) < v:
                need[s] = v

        for b in reads:
            add(b.w)
        for b in writes:
            add(b.w)
            for t in b.r:
                add(t)
        out = []
        for s, v in need.items():
            if self.waited[eng][s] < v:
                out.append((s, v))
                self.waited[eng][s] = v
        return out

    def emit(self, eng, fn, reads=(), writes=(), dma=None):
        waits = self._waits(eng, reads, writes)
        if dma:
            n = DMA_POOLS[dma]
            sem = f"{dma}{self.dma_rot[dma] % n}"
            self.dma_rot[dma] += 1
            inc = 16
            if self.count[sem] > self.waited[eng][sem]:
                waits.append((sem, self.count[sem]))
                self.waited[eng][sem] = self.count[sem]
        else:
            sem = eng
            inc = 1
        self.count[sem] += inc
        tok = (sem, self.count[sem])
        for b in reads:
            b.r.append(tok)
        for b in writes:
            b.w = tok
            b.r = []
        self.ops[eng].append((waits, fn, sem, inc))
        return tok

    def barrier(self):
        snap = dict(self.count)
        for e in ENGINES:
            for s, v in snap.items():
                if v and self.waited[e][s] < v:
                    self.ops[e].append(([(s, v)], None, None, 0))
                    self.waited[e][s] = v

    def finish(self):
        self.barrier()

    def run_block(self):
        plan = self
        with self.nc.Block() as block:
            def mk(ename):
                def body(eng):
                    for waits, fn, sem, inc in plan.ops[ename]:
                        for s, v in waits:
                            eng.wait_ge(plan.sems[s], v)
                        if fn is not None:
                            fn(eng).then_inc(plan.sems[sem], inc)
                return body
            block.tensor(mk("pe"))
            block.scalar(mk("act"))
            block.vector(mk("dve"))
            block.gpsimd(mk("pool"))
            block.sync(mk("sp"))
        self.ops = {e: [] for e in ENGINES}


def bc_mid(ap2d, nmid, width):
    return ap2d.unsqueeze(1).broadcast_to([ap2d.shape[0], nmid, width])


class LNKit:

    def __init__(self, p, a, apsum, T, tag):
        self.p = p
        self.T = T
        self.ones_f = a(tag + "_onesf", [1, 128], F32)
        self.ones_bf = a(tag + "_onesbf", [128, 1], BF16)
        self.cst = a(tag + "_cst", [128, 2], F32)
        self.rows = a(tag + "_rows", [1, 3 * T], F32)
        self.psA = apsum(tag + "_psA", [128, 512])
        self.psB = apsum(tag + "_psB", [128, 512])
        self.psb0 = apsum(tag + "_psb0", [128, 512])
        self.psb1 = apsum(tag + "_psb1", [128, 512])
        (self.b_onef, self.b_onebf, self.b_cst, self.b_rows,
         self.b_psA, self.b_psB, self.b_psb0, self.b_psb1) = (Buf() for _ in range(8))
        p.emit("dve", lambda e: e.memset(self.ones_f[:], 1.0), writes=[self.b_onef])
        p.emit("dve", lambda e: e.memset(self.ones_bf[:], 1.0), writes=[self.b_onebf])
        p.emit("dve", lambda e: e.memset(self.cst[:, 0:1], 1e-5), writes=[self.b_cst])
        p.emit("dve", lambda e: e.memset(self.cst[:, 1:2], -0.5), writes=[self.b_cst])

    def stats(self, cast_dst, sq_dst, b_cs, src_ap, b_src, nch, Dn):
        p, T = self.p, self.T
        p.emit("act", lambda e: e.activation(cast_dst, src_ap, AF.Copy), reads=[b_src], writes=[b_cs])
        p.emit("act", lambda e: e.activation(sq_dst, src_ap, AF.Square), reads=[b_src], writes=[b_cs])

        def mm_sum(psum, src3):
            def fn(e):
                last = None
                for c in range(nch):
                    last = e.matmul(psum[0:1, 0:T], self.ones_bf[:], src3[:, c, 0:T],
                                    start=(c == 0), stop=(c == nch - 1))
                return last
            return fn
        p.emit("pe", mm_sum(self.psA, cast_dst), reads=[b_cs, self.b_onebf], writes=[self.b_psA])
        p.emit("pe", mm_sum(self.psB, sq_dst), reads=[b_cs, self.b_onebf], writes=[self.b_psB])
        rows = self.rows
        p.emit("dve", lambda e: e.tensor_scalar(rows[0:1, 0:T], self.psA[0:1, 0:T], -1.0 / Dn, None, ALU.mult),
               reads=[self.b_psA], writes=[self.b_rows])
        p.emit("dve", lambda e: e.tensor_scalar(rows[0:1, T:2 * T], self.psB[0:1, 0:T], 1.0 / Dn, None, ALU.mult),
               reads=[self.b_psB], writes=[self.b_rows])
        p.emit("dve", lambda e: e.tensor_tensor(rows[0:1, 2 * T:3 * T], rows[0:1, 0:T], rows[0:1, 0:T], ALU.mult),
               reads=[self.b_rows], writes=[self.b_rows])
        p.emit("dve", lambda e: e.tensor_tensor(rows[0:1, T:2 * T], rows[0:1, T:2 * T], rows[0:1, 2 * T:3 * T], ALU.subtract),
               reads=[self.b_rows], writes=[self.b_rows])
        p.emit("act", lambda e: e.activation(rows[0:1, T:2 * T], rows[0:1, T:2 * T], AF.Ln, bias=self.cst[0:1, 0:1]),
               reads=[self.b_rows, self.b_cst], writes=[self.b_rows])
        p.emit("act", lambda e: e.activation(rows[0:1, T:2 * T], rows[0:1, T:2 * T], AF.Exp, scale=self.cst[0:1, 1:2]),
               reads=[self.b_rows, self.b_cst], writes=[self.b_rows])

    def stats_bf(self, src_bf3, sq_dst, b_sq, b_src, nch, Dn):
        p, T = self.p, self.T
        p.emit("act", lambda e: e.activation(sq_dst, src_bf3, AF.Square), reads=[b_src], writes=[b_sq])

        def mm_sum(psum, src3, rd):
            def fn(e):
                last = None
                for c in range(nch):
                    last = e.matmul(psum[0:1, 0:T], self.ones_bf[:], src3[:, c, 0:T],
                                    start=(c == 0), stop=(c == nch - 1))
                return last
            return fn
        p.emit("pe", mm_sum(self.psA, src_bf3, b_src), reads=[b_src, self.b_onebf], writes=[self.b_psA])
        p.emit("pe", mm_sum(self.psB, sq_dst, b_sq), reads=[b_sq, self.b_onebf], writes=[self.b_psB])
        rows = self.rows
        p.emit("dve", lambda e: e.tensor_scalar(rows[0:1, 0:T], self.psA[0:1, 0:T], -1.0 / Dn, None, ALU.mult),
               reads=[self.b_psA], writes=[self.b_rows])
        p.emit("dve", lambda e: e.tensor_scalar(rows[0:1, T:2 * T], self.psB[0:1, 0:T], 1.0 / Dn, None, ALU.mult),
               reads=[self.b_psB], writes=[self.b_rows])
        p.emit("dve", lambda e: e.tensor_tensor(rows[0:1, 2 * T:3 * T], rows[0:1, 0:T], rows[0:1, 0:T], ALU.mult),
               reads=[self.b_rows], writes=[self.b_rows])
        p.emit("dve", lambda e: e.tensor_tensor(rows[0:1, T:2 * T], rows[0:1, T:2 * T], rows[0:1, 2 * T:3 * T], ALU.subtract),
               reads=[self.b_rows], writes=[self.b_rows])
        p.emit("act", lambda e: e.activation(rows[0:1, T:2 * T], rows[0:1, T:2 * T], AF.Ln, bias=self.cst[0:1, 0:1]),
               reads=[self.b_rows, self.b_cst], writes=[self.b_rows])
        p.emit("act", lambda e: e.activation(rows[0:1, T:2 * T], rows[0:1, T:2 * T], AF.Exp, scale=self.cst[0:1, 1:2]),
               reads=[self.b_rows, self.b_cst], writes=[self.b_rows])

    def bcast(self):
        p, T = self.p, self.T
        p.emit("pe", lambda e: e.matmul(self.psb0[:, 0:T], self.ones_f[:], self.rows[0:1, 0:T], start=True, stop=True),
               reads=[self.b_rows, self.b_onef], writes=[self.b_psb0])
        p.emit("pe", lambda e: e.matmul(self.psb1[:, 0:T], self.ones_f[:], self.rows[0:1, T:2 * T], start=True, stop=True),
               reads=[self.b_rows, self.b_onef], writes=[self.b_psb1])




def stage_l0a(p, d, S, T=512):
    nc = p.nc
    H = 8
    W = T + H
    ntiles = S // T
    es = ExitStack()
    a = lambda n, s, t: es.enter_context(nc.sbuf_tensor(n, s, t))
    apsum = lambda n, s: es.enter_context(nc.psum_tensor(n, s, F32))
    kit = LNKit(p, a, apsum, T, "l0")

    w_in_bf = a("l0_w_in_bf", [128, 8 * 3072], BF16)
    w_out_bf = a("l0_w_out_bf", [128, 8 * 1024], BF16)
    bias = a("l0_bias", [128, 64], F32)
    convw = a("l0_convw", [128, 104], F32)
    dmask = a("l0_dmask", [128, 128], F32)
    diags = a("l0_diags", [128, 104 * 128], BF16)
    xt = [a(f"l0_xt{i}", [128, 8, T], F32) for i in range(2)]
    xbf = a("l0_xbf", [128, 8, T], BF16)
    qkv = a("l0_qkv", [128, 24, W], BF16)
    scr = a("l0_scr", [128, 16, W], F32)
    hqkv = a("l0_hqkv", [128, 24, H], BF16)
    hkv = a("l0_hkv", [128, 8, H], BF16)

    b_w_in, b_w_out, b_bias, b_convw, b_dmask, b_diags = (Buf() for _ in range(6))
    b_xt = [Buf(), Buf()]
    b_xbf, b_qkv, b_scrA, b_scrB, b_hqkv, b_hkv = (Buf() for _ in range(6))

    ps = [apsum(f"l0_ps{i}", [128, 512]) for i in range(4)]
    b_ps = [Buf() for _ in range(4)]

    p.emit("sp", lambda e: e.dma_start(bias[:], d["l0a_bias"].ap()), writes=[b_bias], dma="dsp")
    p.emit("sp", lambda e: e.dma_start(convw[:], d["l0a_convw"].ap()), writes=[b_convw], dma="dsp")
    p.emit("sp", lambda e: e.dma_start(dmask[:], d["diagmask"].ap()), writes=[b_dmask], dma="dsp")

    wstage = xt[1]
    for k in range(8):
        wv = wstage.ap().rearrange("p a b -> p (a b)")[:, 0:3072]
        p.emit("sp", lambda e, k=k, wv=wv: e.dma_start(wv, d["l0a_w_in"].ap()[k * 128:(k + 1) * 128, :]),
               reads=[b_xt[1]], writes=[b_xt[1]], dma="dsp")
        p.emit("act", lambda e, k=k, wv=wv: e.activation(
            w_in_bf[:, k * 3072:(k + 1) * 3072], wv, AF.Copy),
            reads=[b_xt[1]], writes=[b_w_in])
    for k in range(8):
        wv = wstage.ap().rearrange("p a b -> p (a b)")[:, 0:1024]
        p.emit("sp", lambda e, k=k, wv=wv: e.dma_start(wv, d["l0a_w_out"].ap()[k * 128:(k + 1) * 128, :]),
               reads=[b_xt[1]], writes=[b_xt[1]], dma="dsp")
        p.emit("act", lambda e, k=k, wv=wv: e.activation(
            w_out_bf[:, k * 1024:(k + 1) * 1024], wv, AF.Copy),
            reads=[b_xt[1]], writes=[b_w_out])

    for i in range(104):
        p.emit("dve", lambda e, i=i: e.tensor_scalar(
            diags[:, i * 128:(i + 1) * 128], dmask[:], convw[:, i:i + 1], None, ALU.mult),
            reads=[b_dmask, b_convw], writes=[b_diags])

    for t in range(ntiles):
        i = t % 2
        t0 = t * T
        p.emit("sp", lambda e, i=i, t0=t0: e.dma_start(
            xt[i].ap(), d["x_in"].ap().rearrange("(c p) s -> p c s", p=128)[:, :, t0:t0 + T]),
            reads=[b_xt[i]], writes=[b_xt[i]], dma="dsp")

        kit.stats(xbf.ap(), qkv.ap()[:, 0:8, 0:T], b_qkv, xt[i].ap(), b_xt[i], 8, 1024)
        kit.bcast()
        p.emit("dve", lambda e, i=i: e.tensor_tensor(
            scr.ap()[:, 8:16, 0:T], xt[i].ap(), bc_mid(kit.psb0.ap()[:, 0:T], 8, T), ALU.add),
            reads=[b_xt[i], kit.b_psb0], writes=[b_scrB])
        p.emit("dve", lambda e: e.tensor_tensor(
            xbf.ap(), scr.ap()[:, 8:16, 0:T], bc_mid(kit.psb1.ap()[:, 0:T], 8, T), ALU.mult),
            reads=[b_scrB, kit.b_psb1], writes=[b_xbf])

        if t == 0:
            p.emit("dve", lambda e: e.memset(qkv.ap()[:, 0:24, 0:H], 0.0), writes=[b_qkv])
        else:
            p.emit("dve", lambda e: e.tensor_copy(qkv.ap()[:, 0:24, 0:H], hqkv.ap()),
                   reads=[b_hqkv], writes=[b_qkv])

        for m in range(24):
            pj = m % 4

            def qkv_mm(e, m=m):
                last = None
                for k in range(8):
                    last = e.matmul(ps[m % 4][:, 0:T],
                                    w_in_bf[:, k * 3072 + m * 128: k * 3072 + (m + 1) * 128],
                                    xbf.ap()[:, k, 0:T], start=(k == 0), stop=(k == 7))
                return last
            p.emit("pe", qkv_mm, reads=[b_xbf, b_w_in], writes=[b_ps[pj]])
            p.emit("act", lambda e, m=m, pj=pj: e.activation(
                qkv.ap()[:, m, H:W], ps[pj][:, 0:T], AF.Identity, bias=bias[:, m:m + 1]),
                reads=[b_ps[pj], b_bias], writes=[b_qkv])

        p.emit("dve", lambda e: e.tensor_copy(hqkv.ap(), qkv.ap()[:, 0:24, T:W]),
               reads=[b_qkv], writes=[b_hqkv])

        for c in range(23, -1, -1):
            pj = c % 4

            def c1(e, c=c, pj=pj):
                last = None
                for k in range(3):
                    last = e.matmul(ps[pj][:, 0:T],
                                    diags[:, (c * 3 + k) * 128:(c * 3 + k + 1) * 128],
                                    qkv.ap()[:, c, H - k:W - k], start=(k == 0), stop=(k == 2))
                return last
            p.emit("pe", c1, reads=[b_qkv, b_diags], writes=[b_ps[pj]])
            if c >= 16:
                p.emit("act", lambda e, c=c, pj=pj: e.activation(
                    scr.ap()[:, c - 8, H:W], ps[pj][:, 0:T], AF.Identity, bias=bias[:, 24 + c:24 + c + 1]),
                    reads=[b_ps[pj], b_bias], writes=[b_scrB])
            elif c >= 8:
                p.emit("act", lambda e, c=c, pj=pj: e.activation(
                    scr.ap()[:, c - 8, H:W], ps[pj][:, 0:T], AF.Identity, bias=bias[:, 24 + c:24 + c + 1]),
                    reads=[b_ps[pj], b_bias], writes=[b_scrA])
            else:
                p.emit("act", lambda e, c=c, pj=pj: e.activation(
                    qkv.ap()[:, 8 + c, H:W], ps[pj][:, 0:T], AF.Identity, bias=bias[:, 24 + c:24 + c + 1]),
                    reads=[b_ps[pj], b_bias], writes=[b_qkv])

        p.emit("dve", lambda e: e.tensor_tensor(
            scr.ap()[:, 8:16, H:W], scr.ap()[:, 0:8, H:W], scr.ap()[:, 8:16, H:W], ALU.mult),
            reads=[b_scrA, b_scrB], writes=[b_scrB])
        p.emit("act", lambda e: e.activation(
            qkv.ap()[:, 16:24, H:W], scr.ap()[:, 8:16, H:W], AF.Copy),
            reads=[b_scrB], writes=[b_qkv])
        if t == 0:
            p.emit("dve", lambda e: e.memset(qkv.ap()[:, 16:24, 0:H], 0.0), writes=[b_qkv])
        else:
            p.emit("dve", lambda e: e.tensor_copy(qkv.ap()[:, 16:24, 0:H], hkv.ap()),
                   reads=[b_hkv], writes=[b_qkv])
        p.emit("dve", lambda e: e.tensor_copy(hkv.ap(), qkv.ap()[:, 16:24, T:W]),
               reads=[b_qkv], writes=[b_hkv])

        for c in range(8):
            pj = c % 4

            def c2(e, c=c, pj=pj):
                last = None
                for k in range(4):
                    last = e.matmul(ps[pj][:, 0:T],
                                    diags[:, (72 + c * 4 + k) * 128:(72 + c * 4 + k + 1) * 128],
                                    qkv.ap()[:, 16 + c, H - k:W - k], start=(k == 0), stop=(k == 3))
                return last
            p.emit("pe", c2, reads=[b_qkv, b_diags], writes=[b_ps[pj]])
            p.emit("act", lambda e, c=c, pj=pj: e.activation(
                scr.ap()[:, c, H:W], ps[pj][:, 0:T], AF.Identity, bias=bias[:, 48 + c:48 + c + 1]),
                reads=[b_ps[pj], b_bias], writes=[b_scrA])

        p.emit("dve", lambda e: e.tensor_tensor(
            qkv.ap()[:, 0:8, H:W], qkv.ap()[:, 8:16, H:W], scr.ap()[:, 0:8, H:W], ALU.mult),
            reads=[b_qkv, b_scrA], writes=[b_qkv])

        for m in range(8):
            pj = m % 4

            def wo_mm(e, m=m, pj=pj):
                last = None
                for k in range(8):
                    last = e.matmul(ps[pj][:, 0:T],
                                    w_out_bf[:, k * 1024 + m * 128: k * 1024 + (m + 1) * 128],
                                    qkv.ap()[:, k, H:W], start=(k == 0), stop=(k == 7))
                return last
            p.emit("pe", wo_mm, reads=[b_qkv, b_w_out], writes=[b_ps[pj]])
            p.emit("act", lambda e, m=m, pj=pj: e.activation(
                scr.ap()[:, 8 + m, 0:T], ps[pj][:, 0:T], AF.Identity, bias=bias[:, 56 + m:56 + m + 1]),
                reads=[b_ps[pj], b_bias], writes=[b_scrB])

        kit.stats(qkv.ap()[:, 8:16, 0:T], qkv.ap()[:, 16:24, 0:T], b_qkv,
                 scr.ap()[:, 8:16, 0:T], b_scrB, 8, 1024)
        kit.bcast()
        p.emit("dve", lambda e: e.tensor_tensor(
            scr.ap()[:, 8:16, 0:T], scr.ap()[:, 8:16, 0:T], bc_mid(kit.psb0.ap()[:, 0:T], 8, T), ALU.add),
            reads=[b_scrB, kit.b_psb0], writes=[b_scrB])
        p.emit("dve", lambda e: e.tensor_tensor(
            scr.ap()[:, 8:16, 0:T], scr.ap()[:, 8:16, 0:T], bc_mid(kit.psb1.ap()[:, 0:T], 8, T), ALU.mult),
            reads=[b_scrB, kit.b_psb1], writes=[b_scrB])
        p.emit("dve", lambda e, i=i: e.tensor_tensor(
            scr.ap()[:, 8:16, 0:T], scr.ap()[:, 8:16, 0:T], xt[i].ap(), ALU.add),
            reads=[b_scrB, b_xt[i]], writes=[b_scrB])
        p.emit("pool", lambda e, t0=t0: e.dma_start(
            d["x_out"].ap().rearrange("(c p) s -> p c s", p=128)[:, :, t0:t0 + T],
            scr.ap()[:, 8:16, 0:T]),
            reads=[b_scrB], dma="dpool")

    p.barrier()
    p.run_block()
    es.close()



def stage_l0f(p, d, S, T=512):
    nc = p.nc
    ntiles = S // T
    es = ExitStack()
    a = lambda n, s, t: es.enter_context(nc.sbuf_tensor(n, s, t))
    apsum = lambda n, s: es.enter_context(nc.psum_tensor(n, s, F32))
    kit = LNKit(p, a, apsum, T, "f0")

    w2bf = a("f0_w2", [128, 32 * 1024], BF16)
    bias = a("f0_bias", [128, 40], F32)
    xt = [a(f"f0_xt{i}", [128, 8, T], F32) for i in range(2)]
    xbf = a("f0_xbf", [128, 8, T], BF16)
    h = a("f0_h", [128, 32, T], BF16)
    fbuf = a("f0_fbuf", [128, 8, T], F32)
    w1s = [a(f"f0_w1s{i}", [128, 8, 128], F32) for i in range(2)]
    w1b = [a(f"f0_w1b{i}", [128, 8, 128], BF16) for i in range(2)]

    b_w2, b_bias, b_xbf, b_h, b_f = (Buf() for _ in range(5))
    b_xt = [Buf(), Buf()]
    b_w1s = [Buf(), Buf()]
    b_w1b = [Buf(), Buf()]
    ps = [apsum(f"f0_ps{i}", [128, 512]) for i in range(4)]
    b_ps = [Buf() for _ in range(4)]

    p.emit("sp", lambda e: e.dma_start(bias[:], d["l0f_bias"].ap()), writes=[b_bias], dma="dsp")
    wstage = xt[1]
    for k in range(32):
        wv = wstage.ap().rearrange("p a b -> p (a b)")[:, 0:1024]
        p.emit("sp", lambda e, k=k, wv=wv: e.dma_start(wv, d["l0f_w2"].ap()[k * 128:(k + 1) * 128, :]),
               reads=[b_xt[1]], writes=[b_xt[1]], dma="dsp")
        p.emit("act", lambda e, k=k, wv=wv: e.activation(
            w2bf[:, k * 1024:(k + 1) * 1024], wv, AF.Copy),
            reads=[b_xt[1]], writes=[b_w2])

    for t in range(ntiles):
        i = t % 2
        t0 = t * T
        p.emit("sp", lambda e, i=i, t0=t0: e.dma_start(
            xt[i].ap(), d["x_in"].ap().rearrange("(c p) s -> p c s", p=128)[:, :, t0:t0 + T]),
            reads=[b_xt[i]], writes=[b_xt[i]], dma="dsp")

        kit.stats(xbf.ap(), h.ap()[:, 0:8, 0:T], b_h, xt[i].ap(), b_xt[i], 8, 1024)
        kit.bcast()
        p.emit("dve", lambda e, i=i: e.tensor_tensor(
            fbuf.ap(), xt[i].ap(), bc_mid(kit.psb0.ap()[:, 0:T], 8, T), ALU.add),
            reads=[b_xt[i], kit.b_psb0], writes=[b_f])
        p.emit("dve", lambda e: e.tensor_tensor(
            xbf.ap(), fbuf.ap(), bc_mid(kit.psb1.ap()[:, 0:T], 8, T), ALU.mult),
            reads=[b_f, kit.b_psb1], writes=[b_xbf])

        for m in range(32):
            pb = m % 2
            pj = m % 4
            p.emit("sp", lambda e, m=m, pb=pb: e.dma_start(
                w1s[pb].ap(), d["l0f_w1"].ap().rearrange("(k p) m -> p k m", p=128)[:, :, m * 128:(m + 1) * 128]),
                reads=[b_w1s[pb]], writes=[b_w1s[pb]], dma="dsp")
            p.emit("act", lambda e, pb=pb: e.activation(w1b[pb].ap(), w1s[pb].ap(), AF.Copy),
                   reads=[b_w1s[pb]], writes=[b_w1b[pb]])

            def w1_mm(e, pb=pb, pj=pj):
                last = None
                for k in range(8):
                    last = e.matmul(ps[pj][:, 0:T], w1b[pb].ap()[:, k, :],
                                    xbf.ap()[:, k, 0:T], start=(k == 0), stop=(k == 7))
                return last
            p.emit("pe", w1_mm, reads=[b_w1b[pb], b_xbf], writes=[b_ps[pj]])
            p.emit("act", lambda e, m=m, pj=pj: e.activation(
                h.ap()[:, m, 0:T], ps[pj][:, 0:T], AF.Relu, bias=bias[:, m:m + 1]),
                reads=[b_ps[pj], b_bias], writes=[b_h])

        for m in range(8):
            pj = m % 4

            def w2_mm(e, m=m, pj=pj):
                last = None
                for k in range(32):
                    last = e.matmul(ps[pj][:, 0:T],
                                    w2bf[:, k * 1024 + m * 128: k * 1024 + (m + 1) * 128],
                                    h.ap()[:, k, 0:T], start=(k == 0), stop=(k == 31))
                return last
            p.emit("pe", w2_mm, reads=[b_h, b_w2], writes=[b_ps[pj]])
            p.emit("act", lambda e, m=m, pj=pj: e.activation(
                fbuf.ap()[:, m, 0:T], ps[pj][:, 0:T], AF.Identity, bias=bias[:, 32 + m:32 + m + 1]),
                reads=[b_ps[pj], b_bias], writes=[b_f])

        kit.stats(h.ap()[:, 0:8, 0:T], h.ap()[:, 8:16, 0:T], b_h, fbuf.ap(), b_f, 8, 1024)
        kit.bcast()
        p.emit("dve", lambda e: e.tensor_tensor(
            fbuf.ap(), fbuf.ap(), bc_mid(kit.psb0.ap()[:, 0:T], 8, T), ALU.add),
            reads=[b_f, kit.b_psb0], writes=[b_f])
        p.emit("dve", lambda e: e.tensor_tensor(
            fbuf.ap(), fbuf.ap(), bc_mid(kit.psb1.ap()[:, 0:T], 8, T), ALU.mult),
            reads=[b_f, kit.b_psb1], writes=[b_f])
        p.emit("dve", lambda e, i=i: e.tensor_tensor(
            fbuf.ap(), fbuf.ap(), xt[i].ap(), ALU.add),
            reads=[b_f, b_xt[i]], writes=[b_f])
        p.emit("pool", lambda e, t0=t0: e.dma_start(
            d["x_out"].ap().rearrange("(c p) s -> p c s", p=128)[:, :, t0:t0 + T],
            fbuf.ap()),
            reads=[b_f], dma="dpool")

    p.barrier()
    p.run_block()
    es.close()



def stage_l1a(p, d, S, T=512):
    nc = p.nc
    NCH = S // T
    NL = 4

    es0 = ExitStack()
    a0 = lambda n_, s_, t_: es0.enter_context(nc.sbuf_tensor(n_, s_, t_))
    apsum = lambda n_, s_: es0.enter_context(nc.psum_tensor(n_, s_, F32))
    kit = LNKit(p, a0, apsum, T, "m0")
    ps = [apsum(f"m_ps{i}", [128, 512]) for i in range(4)]
    b_ps = [Buf() for _ in range(4)]
    import itertools
    prot = itertools.cycle(range(4))

    xa = a0("m_xa", [128, 8, T], F32)
    xnb = a0("m_xnb", [128, 8, T], BF16)
    scr8a = a0("m_scr8a", [128, 8, T], F32)
    scr8b = a0("m_scr8b", [128, 8, T], F32)
    c2acc = a0("m_c2acc", [128, T], F32)
    spscr = a0("m_spscr", [128, T], F32)
    b_xa, b_xnb, b_sa, b_sb, b_c2, b_sp = (Buf() for _ in range(6))

    xaps = [d["xa_a"].ap().rearrange("(c p) s -> p c s", p=128),
            d["xa_b"].ap().rearrange("(c p) s -> p c s", p=128)]
    hs_ap = d["hs_save"].ap().rearrange("(c p) s -> p c s", p=128)
    xin_ap = d["x_in"].ap().rearrange("(c p) s -> p c s", p=128)
    xout_ap = d["x_out"].ap().rearrange("(c p) s -> p c s", p=128)

    def apply_plain(src_ap, b_src, dst_ap, b_dst, scr_ap, b_scr):
        p.emit("dve", lambda e: e.tensor_tensor(
            scr_ap, src_ap, bc_mid(kit.psb0.ap()[:, 0:T], 8, T), ALU.add),
            reads=[b_src, kit.b_psb0], writes=[b_scr])
        p.emit("dve", lambda e: e.tensor_tensor(
            dst_ap, scr_ap, bc_mid(kit.psb1.ap()[:, 0:T], 8, T), ALU.mult),
            reads=[b_scr, kit.b_psb1], writes=[b_dst])

    with ExitStack() as es1:
        a1 = lambda n_, s_, t_: es1.enter_context(nc.sbuf_tensor(n_, s_, t_))
        sqs = a1("m_sqs", [128, 8, T], BF16)
        cbf = a1("m_cbf", [128, 8, T], BF16)
        inn = a1("m_inn", [128, 16], F32)
        b_sq, b_cb, b_inn = Buf(), Buf(), Buf()
        p.emit("sp", lambda e: e.dma_start(inn[:], d["l1a_innorm"].ap()), writes=[b_inn], dma="dsp")
        for t in range(NCH):
            t0 = t * T
            p.emit("sp", lambda e, t0=t0: e.dma_start(xa.ap(), xin_ap[:, :, t0:t0 + T]),
                   reads=[b_xa], writes=[b_xa], dma="dsp")
            kit.stats(cbf.ap(), sqs.ap(), b_cb, xa.ap(), b_xa, 8, 1024)
            kit.bcast()
            apply_plain(xa.ap(), b_xa, xa.ap(), b_xa, scr8a.ap(), b_sa)
            p.emit("pool", lambda e, t0=t0: e.dma_start(hs_ap[:, :, t0:t0 + T], xa.ap()),
                   reads=[b_xa], dma="dpool")
            kit.stats(cbf.ap(), sqs.ap(), b_cb, xa.ap(), b_xa, 8, 1024)
            kit.bcast()
            apply_plain(xa.ap(), b_xa, scr8b.ap(), b_sb, scr8a.ap(), b_sa)
            for c in range(8):
                p.emit("dve", lambda e, c=c: e.tensor_scalar(
                    xa.ap()[:, c, 0:T], scr8b.ap()[:, c, 0:T],
                    inn[:, c:c + 1], inn[:, 8 + c:8 + c + 1], ALU.mult, ALU.add),
                    reads=[b_sb, b_inn], writes=[b_xa])
            p.emit("pool", lambda e, t0=t0: e.dma_start(xaps[0][:, :, t0:t0 + T], xa.ap()),
                   reads=[b_xa], dma="dpool")
        p.barrier()
        p.run_block()

    with ExitStack() as es2:
        a2_ = lambda n_, s_, t_: es2.enter_context(nc.sbuf_tensor(n_, s_, t_))
        cols = a2_("m_cols", [128, 336], F32)
        convw = a2_("m_convw", [128, 512], F32)
        convb = a2_("m_convb", [128, 128], F32)
        ncst = a2_("m_ncst", [128, 18], F32)
        xprj = a2_("m_xprj", [128, 16 * 33], BF16)
        dtw = [a2_(f"m_dtw{i}", [1, 128], F32) for i in range(2)]
        wstg = scr8a.ap().rearrange("p a b -> p (a b)")[:, 0:2048].rearrange("p (a b) -> p a b", a=16)
        ipc = [a2_(f"m_ipc{i}", [128, 8, 128], BF16) for i in range(2)]
        opc = [a2_(f"m_opc{i}", [128, 16, 128], BF16) for i in range(1)]
        xi = a2_("m_xi", [128, 16, T + 8], BF16)
        sg = a2_("m_sg", [128, 16, T], BF16)
        Bbc = a2_("m_Bbc", [128, 16, T], BF16)
        Cbc = a2_("m_Cbc", [128, 16, T], BF16)
        BC = a2_("m_BC", [33, T], F32)
        brow = a2_("m_brow", [1, 2 * T], F32)
        ybuf = a2_("m_ybuf", [128, 16, T], BF16)
        hx0 = a2_("m_hx0", [128, 16, 8], BF16)
        hx1 = a2_("m_hx1", [128, 16, 8], BF16)
        hcar = a2_("m_hcar", [128, 16, 16], F32)
        tmp8 = a2_("m_tmp8", [128, 8], F32)
        bt2 = [a2_(f"m_bt2{i}", [128, 8, T], BF16) for i in range(2)]
        ht2 = [a2_(f"m_ht2{i}", [128, 8, T], BF16) for i in range(2)]
        dt2 = [a2_(f"m_dt2{i}", [128, T], F32) for i in range(2)]
        u2 = [a2_(f"m_u2{i}", [128, T], BF16) for i in range(2)]

        (b_cols, b_convw, b_convb, b_ncst, b_dmask_unused, b_diag1_unused, b_xprj, b_dtw_unused,
         b_wstg, b_xi, b_sg, b_Bbc, b_Cbc, b_BC, b_brow, b_ybuf, b_hx0,
         b_hx1, b_hcar, b_tmp8) = (Buf() for _ in range(20))
        b_ipc = [Buf(), Buf()]
        b_opc = [Buf()]
        b_bt2 = [Buf(), Buf()]
        b_ht2 = [Buf(), Buf()]
        b_dt2 = [Buf(), Buf()]
        b_u2 = [Buf(), Buf()]
        b_a2 = [b_sa, b_sb]
        at2 = [scr8a, scr8b]

        p.emit("sp", lambda e: e.dma_start(cols[:], d["l1a_cols"].ap()), writes=[b_cols], dma="dsp")
        p.emit("sp", lambda e: e.dma_start(convw[:], d["l1a_convw"].ap()), writes=[b_convw], dma="dsp")
        p.emit("sp", lambda e: e.dma_start(convb[:], d["l1a_convb"].ap()), writes=[b_convb], dma="dsp")
        for n_ in range(16):
            p.emit("dve", lambda e, n_=n_: e.memset(ncst[:, n_:n_ + 1], -float(n_ + 1)), writes=[b_ncst])
        p.emit("dve", lambda e: e.memset(ncst[:, 16:17], 1.0), writes=[b_ncst])
        p.emit("dve", lambda e: e.memset(ncst[:, 17:18], -1.0), writes=[b_ncst])

        for l in range(NL):
            src_ap = xaps[l % 2]
            dst_ap = xaps[(l + 1) % 2]
            COL = 16 + l * 80

            for m in range(32):
                p.emit("sp", lambda e, m=m, l=l: e.dma_start(
                    wstg[:, 0:8, :],
                    d["l1a_inproj"].ap()[l].rearrange("(k p) m -> p k m", p=128)[:, :, m * 128:(m + 1) * 128]),
                    reads=[b_wstg, b_sa], writes=[b_wstg, b_sa], dma="dsp")
                p.emit("act", lambda e: e.activation(ipc[0].ap(), wstg[:, 0:8, :], AF.Copy),
                       reads=[b_wstg], writes=[b_ipc[0]])
                p.emit("pool", lambda e, m=m, l=l: e.dma_start(
                    d["wbf_ip"].ap()[l, m], ipc[0].ap().rearrange("p a b -> p (a b)")),
                    reads=[b_ipc[0]], dma="dpool")
            for m in range(8):
                p.emit("sp", lambda e, m=m, l=l: e.dma_start(
                    wstg,
                    d["l1a_outproj"].ap()[l].rearrange("(k p) m -> p k m", p=128)[:, :, m * 128:(m + 1) * 128]),
                    reads=[b_wstg, b_sa], writes=[b_wstg, b_sa], dma="dsp")
                p.emit("act", lambda e: e.activation(opc[0].ap(), wstg, AF.Copy),
                       reads=[b_wstg], writes=[b_opc[0]])
                p.emit("pool", lambda e, m=m, l=l: e.dma_start(
                    d["wbf_op"].ap()[l, m], opc[0].ap().rearrange("p a b -> p (a b)")),
                    reads=[b_opc[0]], dma="dpool")
            p.emit("sp", lambda e, l=l: e.dma_start(
                wstg[:, 0:16, 0:33],
                d["l1a_xproj"].ap()[l].rearrange("(k p) m -> p k m", p=128)),
                reads=[b_wstg, b_sa], writes=[b_wstg, b_sa], dma="dsp")
            p.emit("act", lambda e: e.activation(
                xprj.ap().rearrange("p (a b) -> p a b", a=16), wstg[:, 0:16, 0:33], AF.Copy),
                reads=[b_wstg], writes=[b_xprj])
            p.emit("dve", lambda e: e.memset(hcar.ap(), 0.0), writes=[b_hcar])

            for ci in range(NCH):
                t0 = ci * T
                p.emit("sp", lambda e, t0=t0, src_ap=src_ap: e.dma_start(
                    xa.ap(), src_ap[:, :, t0:t0 + T]),
                    reads=[b_xa], writes=[b_xa], dma="dsp")
                kit.stats(ybuf.ap()[:, 0:8, 0:T], ybuf.ap()[:, 8:16, 0:T], b_ybuf,
                          xa.ap(), b_xa, 8, 1024)
                kit.bcast()
                apply_plain(xa.ap(), b_xa, at2[1].ap(), b_a2[1], at2[0].ap(), b_a2[0])
                for c in range(8):
                    p.emit("dve", lambda e, c=c: e.tensor_scalar(
                        xnb.ap()[:, c, 0:T], at2[1].ap()[:, c, 0:T],
                        cols[:, COL + c:COL + c + 1], cols[:, COL + 8 + c:COL + 9 + c],
                        ALU.mult, ALU.add),
                        reads=[b_a2[1], b_cols], writes=[b_xnb])

                for m in range(32):
                    pb = m % 2
                    pj = next(prot)
                    p.emit("sp", lambda e, m=m, l=l, pb=pb: e.dma_start(
                        ipc[pb].ap().rearrange("p a b -> p (a b)"), d["wbf_ip"].ap()[l, m]),
                        reads=[b_ipc[pb]], writes=[b_ipc[pb]], dma="dsp")

                    def ip_mm(e, pb=pb, pj=pj):
                        last = None
                        for k in range(8):
                            last = e.matmul(ps[pj][:, 0:T], ipc[pb].ap()[:, k, :],
                                            xnb.ap()[:, k, 0:T], start=(k == 0), stop=(k == 7))
                        return last
                    p.emit("pe", ip_mm, reads=[b_ipc[pb], b_xnb], writes=[b_ps[pj]])
                    if m < 16:
                        p.emit("act", lambda e, m=m, pj=pj: e.activation(
                            xi.ap()[:, m, 8:T + 8], ps[pj][:, 0:T], AF.Silu),
                            reads=[b_ps[pj]], writes=[b_xi])
                    else:
                        p.emit("act", lambda e, m=m, pj=pj: e.activation(
                            sg.ap()[:, m - 16, 0:T], ps[pj][:, 0:T], AF.Sigmoid),
                            reads=[b_ps[pj]], writes=[b_sg])

                if ci == 0:
                    p.emit("dve", lambda e: e.memset(xi.ap()[:, :, 0:8], 0.0), writes=[b_xi])
                else:
                    p.emit("dve", lambda e: e.tensor_copy(xi.ap()[:, :, 0:8], hx0.ap()),
                           reads=[b_hx0], writes=[b_xi])
                p.emit("dve", lambda e: e.tensor_copy(hx0.ap(), xi.ap()[:, :, T:T + 8]),
                       reads=[b_xi], writes=[b_hx0])
                for dc in range(16):
                    wb1 = 128 * l + dc * 4
                    p.emit("dve", lambda e, dc=dc, wb1=wb1: e.tensor_scalar(
                        c2acc.ap(), xi.ap()[:, dc, 8:T + 8], convw[:, wb1:wb1 + 1], None, ALU.mult),
                        reads=[b_xi, b_convw], writes=[b_c2])
                    for k in range(1, 4):
                        p.emit("dve", lambda e, dc=dc, k=k, wb1=wb1: e.scalar_tensor_tensor(
                            c2acc.ap(), xi.ap()[:, dc, 8 - k:T + 8 - k], convw[:, wb1 + k:wb1 + k + 1],
                            c2acc.ap(), ALU.mult, ALU.add),
                            reads=[b_xi, b_convw, b_c2], writes=[b_c2])
                    p.emit("act", lambda e, dc=dc, l=l: e.activation(
                        xi.ap()[:, dc, 8:T + 8], c2acc.ap(), AF.Silu,
                        bias=convb[:, 32 * l + dc:32 * l + dc + 1]),
                        reads=[b_c2, b_convb], writes=[b_xi])

                if ci == 0:
                    p.emit("dve", lambda e: e.memset(xi.ap()[:, :, 0:8], 0.0), writes=[b_xi])
                else:
                    p.emit("dve", lambda e: e.tensor_copy(xi.ap()[:, :, 0:8], hx1.ap()),
                           reads=[b_hx1], writes=[b_xi])
                p.emit("dve", lambda e: e.tensor_copy(hx1.ap(), xi.ap()[:, :, T:T + 8]),
                       reads=[b_xi], writes=[b_hx1])
                for dc in range(16):
                    wbase = 128 * l + 64 + dc * 4
                    p.emit("dve", lambda e, dc=dc, wbase=wbase: e.tensor_scalar(
                        c2acc.ap(), xi.ap()[:, dc, 8:T + 8], convw[:, wbase:wbase + 1], None, ALU.mult),
                        reads=[b_xi, b_convw], writes=[b_c2])
                    for k in range(1, 4):
                        p.emit("dve", lambda e, dc=dc, k=k, wbase=wbase: e.scalar_tensor_tensor(
                            c2acc.ap(), xi.ap()[:, dc, 8 - k:T + 8 - k], convw[:, wbase + k:wbase + k + 1],
                            c2acc.ap(), ALU.mult, ALU.add),
                            reads=[b_xi, b_convw, b_c2], writes=[b_c2])
                    p.emit("act", lambda e, dc=dc, l=l: e.activation(
                        xi.ap()[:, dc, 8:T + 8], c2acc.ap(), AF.Silu,
                        bias=convb[:, 32 * l + 16 + dc:32 * l + 17 + dc]),
                        reads=[b_c2, b_convb], writes=[b_xi])

                pj = next(prot)

                def xp_mm(e, pj=pj):
                    last = None
                    for k in range(16):
                        last = e.matmul(ps[pj][0:33, 0:T], xprj[:, k * 33:(k + 1) * 33],
                                        xi.ap()[:, k, 8:T + 8], start=(k == 0), stop=(k == 15))
                    return last
                p.emit("pe", xp_mm, reads=[b_xi, b_xprj], writes=[b_ps[pj]])
                p.emit("act", lambda e, pj=pj: e.activation(BC.ap(), ps[pj][0:33, 0:T], AF.Copy),
                       reads=[b_ps[pj]], writes=[b_BC])

                b_brow2 = [b_brow, Buf()]
                for which, dstb, bufb in (("B", Bbc, b_Bbc), ("C", Cbc, b_Cbc)):
                    base = 1 if which == "B" else 17
                    for n_ in range(16):
                        rp = n_ % 2
                        p.emit("sp", lambda e, base=base, n_=n_, rp=rp: e.dma_start(
                            brow[0:1, rp * T:(rp + 1) * T], BC.ap()[base + n_:base + n_ + 1, 0:T]),
                            reads=[b_BC, b_brow2[rp]], writes=[b_brow2[rp]], dma="dsp")
                        pj = next(prot)
                        p.emit("pe", lambda e, pj=pj, rp=rp: e.matmul(
                            ps[pj][:, 0:T], kit.ones_f[:], brow[0:1, rp * T:(rp + 1) * T],
                            start=True, stop=True),
                            reads=[b_brow2[rp], kit.b_onef], writes=[b_ps[pj]])
                        p.emit("act", lambda e, pj=pj, n_=n_, dstb=dstb: e.activation(
                            dstb.ap()[:, n_, 0:T], ps[pj][:, 0:T], AF.Copy),
                            reads=[b_ps[pj]], writes=[bufb])

                b_dtw2 = [Buf(), Buf()]
                for dc in range(16):
                    par = dc % 2
                    pj = next(prot)
                    p.emit("sp", lambda e, dc=dc, l=l, par=par: e.dma_start(
                        dtw[par].ap(), d["l1a_dtw"].ap()[l:l + 1, dc * 128:(dc + 1) * 128]),
                        reads=[b_dtw2[par]], writes=[b_dtw2[par]], dma="dsp")
                    p.emit("pe", lambda e, par=par, pj=pj: e.matmul(
                        ps[pj][:, 0:T], dtw[par].ap(), BC.ap()[0:1, 0:T],
                        start=True, stop=True),
                        reads=[b_dtw2[par], b_BC], writes=[b_ps[pj]])
                    p.emit("act", lambda e, dc=dc, pj=pj, COL=COL, par=par: e.activation(
                        dt2[par].ap(), ps[pj][:, 0:T], AF.Identity,
                        bias=cols[:, COL + 16 + dc:COL + 17 + dc]),
                        reads=[b_ps[pj], b_cols], writes=[b_dt2[par]])
                    p.emit("dve", lambda e, par=par: e.tensor_scalar(
                        spscr.ap(), dt2[par].ap(), -1.0, None, ALU.mult),
                        reads=[b_dt2[par]], writes=[b_sp])
                    p.emit("dve", lambda e, par=par: e.tensor_tensor(
                        spscr.ap(), spscr.ap(), dt2[par].ap(), ALU.max),
                        reads=[b_sp, b_dt2[par]], writes=[b_sp])
                    p.emit("act", lambda e: e.activation(
                        spscr.ap(), spscr.ap(), AF.Exp, scale=ncst[:, 17:18]),
                        reads=[b_sp, b_ncst], writes=[b_sp])
                    p.emit("act", lambda e: e.activation(
                        spscr.ap(), spscr.ap(), AF.Ln, bias=ncst[:, 16:17]),
                        reads=[b_sp, b_ncst], writes=[b_sp])
                    p.emit("dve", lambda e, par=par: e.tensor_scalar(
                        dt2[par].ap(), dt2[par].ap(), 0.0, None, ALU.max),
                        reads=[b_dt2[par]], writes=[b_dt2[par]])
                    p.emit("dve", lambda e, par=par: e.tensor_tensor(
                        dt2[par].ap(), dt2[par].ap(), spscr.ap(), ALU.add),
                        reads=[b_dt2[par], b_sp], writes=[b_dt2[par]])
                    p.emit("dve", lambda e, dc=dc, par=par: e.tensor_tensor(
                        u2[par].ap(), dt2[par].ap(), xi.ap()[:, dc, 8:T + 8], ALU.mult),
                        reads=[b_dt2[par], b_xi], writes=[b_u2[par]])

                    for hh in range(2):
                        sp2 = hh
                        av = at2[sp2]
                        bv = bt2[sp2]
                        hv = ht2[sp2]
                        b_av, b_bv, b_hv = b_a2[sp2], b_bt2[sp2], b_ht2[sp2]
                        for j in range(8):
                            nidx = hh * 8 + j
                            p.emit("dve", lambda e, j=j, nidx=nidx, par=par, av=av: e.tensor_scalar(
                                av.ap()[:, j, 0:T], dt2[par].ap(), ncst[:, nidx:nidx + 1], 1.0,
                                ALU.mult, ALU.add),
                                reads=[b_dt2[par], b_ncst], writes=[b_av])
                        p.emit("dve", lambda e, par=par, hh=hh, bv=bv: e.tensor_tensor(
                            bv.ap(), bc_mid(u2[par].ap(), 8, T), Bbc.ap()[:, hh * 8:hh * 8 + 8, 0:T],
                            ALU.mult),
                            reads=[b_u2[par], b_Bbc], writes=[b_bv])
                        p.emit("dve", lambda e, dc=dc, hh=hh, av=av: e.tensor_tensor(
                            tmp8.ap(), av.ap()[:, :, 0].squeeze(2) if False else av.ap()[:, :, 0],
                            hcar.ap()[:, dc, hh * 8:hh * 8 + 8], ALU.mult),
                            reads=[b_av, b_hcar], writes=[b_tmp8])
                        p.emit("dve", lambda e, bv=bv: e.tensor_tensor(
                            bv.ap()[:, :, 0], tmp8.ap(), bv.ap()[:, :, 0], ALU.add),
                            reads=[b_tmp8, b_bv], writes=[b_bv])
                        p.emit("dve", lambda e, av=av: e.memset(av.ap()[:, :, 0:1], 0.0),
                               reads=[b_av], writes=[b_av])
                        p.emit("dve", lambda e, av=av, bv=bv, hv=hv: e.tensor_tensor_scan(
                            hv.ap().rearrange("p a b -> p (a b)"),
                            av.ap().rearrange("p a b -> p (a b)"),
                            bv.ap().rearrange("p a b -> p (a b)"),
                            0.0, ALU.mult, ALU.add),
                            reads=[b_av, b_bv], writes=[b_hv])
                        p.emit("dve", lambda e, dc=dc, hh=hh, hv=hv: e.tensor_copy(
                            hcar.ap()[:, dc, hh * 8:hh * 8 + 8], hv.ap()[:, :, T - 1]),
                            reads=[b_hv], writes=[b_hcar])
                        p.emit("dve", lambda e, hh=hh, hv=hv: e.tensor_tensor(
                            hv.ap(), hv.ap(), Cbc.ap()[:, hh * 8:hh * 8 + 8, 0:T], ALU.mult),
                            reads=[b_hv, b_Cbc], writes=[b_hv])
                        p.emit("pool", lambda e, av=av, hv=hv: e.tensor_tensor(
                            av.ap()[:, 0:4, 0:T], hv.ap()[:, 0:4, 0:T], hv.ap()[:, 4:8, 0:T], ALU.add),
                            reads=[b_hv], writes=[b_av])
                        p.emit("dve", lambda e, av=av: e.tensor_tensor(
                            av.ap()[:, 0:2, 0:T], av.ap()[:, 0:2, 0:T], av.ap()[:, 2:4, 0:T], ALU.add),
                            reads=[b_av], writes=[b_av])
                        p.emit("dve", lambda e, av=av: e.tensor_tensor(
                            av.ap()[:, 0:1, 0:T], av.ap()[:, 0:1, 0:T], av.ap()[:, 1:2, 0:T], ALU.add),
                            reads=[b_av], writes=[b_av])
                    p.emit("dve", lambda e: e.tensor_tensor(
                        c2acc.ap(), at2[0].ap()[:, 0, 0:T], at2[1].ap()[:, 0, 0:T], ALU.add),
                        reads=[b_a2[0], b_a2[1]], writes=[b_c2])
                    p.emit("dve", lambda e, dc=dc, COL=COL: e.scalar_tensor_tensor(
                        ybuf.ap()[:, dc, 0:T], xi.ap()[:, dc, 8:T + 8],
                        cols[:, COL + 32 + dc:COL + 33 + dc], c2acc.ap(), ALU.mult, ALU.add),
                        reads=[b_xi, b_cols, b_c2], writes=[b_ybuf])

                kit.stats_bf(ybuf.ap(), Bbc.ap(), b_Bbc, b_ybuf, 16, 2048)
                kit.bcast()
                for hh in range(2):
                    av, b_av = at2[hh], b_a2[hh]
                    p.emit("dve", lambda e, hh=hh, av=av: e.tensor_tensor(
                        av.ap(), ybuf.ap()[:, hh * 8:hh * 8 + 8, 0:T],
                        bc_mid(kit.psb0.ap()[:, 0:T], 8, T), ALU.add),
                        reads=[b_ybuf, kit.b_psb0], writes=[b_av])
                    p.emit("dve", lambda e, av=av: e.tensor_tensor(
                        av.ap(), av.ap(), bc_mid(kit.psb1.ap()[:, 0:T], 8, T), ALU.mult),
                        reads=[b_av, kit.b_psb1], writes=[b_av])
                    for j in range(8):
                        dc = hh * 8 + j
                        p.emit("dve", lambda e, j=j, dc=dc, av=av, COL=COL: e.tensor_scalar(
                            av.ap()[:, j, 0:T], av.ap()[:, j, 0:T],
                            cols[:, COL + 48 + dc:COL + 49 + dc], cols[:, COL + 64 + dc:COL + 65 + dc],
                            ALU.mult, ALU.add),
                            reads=[b_av, b_cols], writes=[b_av])
                    p.emit("dve", lambda e, hh=hh, av=av: e.tensor_tensor(
                        sg.ap()[:, hh * 8:hh * 8 + 8, 0:T], av.ap(),
                        sg.ap()[:, hh * 8:hh * 8 + 8, 0:T], ALU.mult),
                        reads=[b_av, b_sg], writes=[b_sg])

                for m in range(8):
                    pj = next(prot)
                    p.emit("sp", lambda e, m=m, l=l: e.dma_start(
                        opc[0].ap().rearrange("p a b -> p (a b)"), d["wbf_op"].ap()[l, m]),
                        reads=[b_opc[0]], writes=[b_opc[0]], dma="dsp")

                    def op_mm(e, pj=pj):
                        last = None
                        for k in range(16):
                            last = e.matmul(ps[pj][:, 0:T], opc[0].ap()[:, k, :],
                                            sg.ap()[:, k, 0:T], start=(k == 0), stop=(k == 15))
                        return last
                    p.emit("pe", op_mm, reads=[b_opc[0], b_sg], writes=[b_ps[pj]])
                    p.emit("dve", lambda e, m=m, pj=pj: e.tensor_tensor(
                        xa.ap()[:, m, 0:T], ps[pj][:, 0:T], xa.ap()[:, m, 0:T], ALU.add),
                        reads=[b_ps[pj], b_xa], writes=[b_xa])
                p.emit("pool", lambda e, t0=t0, dst_ap=dst_ap: e.dma_start(
                    dst_ap[:, :, t0:t0 + T], xa.ap()),
                    reads=[b_xa], dma="dpool")


        p.barrier()
        p.run_block()

    with ExitStack() as es3:
        a3 = lambda n_, s_, t_: es3.enter_context(nc.sbuf_tensor(n_, s_, t_))
        op1 = a3("m_op1", [128, 8 * 1024], BF16)
        op2 = a3("m_op2", [128, 8 * 1024], BF16)
        opb = a3("m_opb", [128, 16], F32)
        hst = a3("m_hst", [128, 8, T], F32)
        obuf = a3("m_obuf", [128, 8, T], F32)
        sqs = a3("m_sqs3", [128, 8, T], BF16)
        cbf = a3("m_cbf3", [128, 8, T], BF16)
        b_op1, b_op2, b_opb, b_hst, b_ob, b_sq, b_cb = (Buf() for _ in range(7))

        p.emit("sp", lambda e: e.dma_start(opb[:], d["l1a_opb"].ap()), writes=[b_opb], dma="dsp")
        wstg3 = a3("m_wstg3", [128, 1024], F32)
        b_w3 = Buf()
        for k in range(8):
            p.emit("sp", lambda e, k=k: e.dma_start(wstg3.ap(), d["l1a_op1"].ap()[k * 128:(k + 1) * 128, :]),
                   reads=[b_w3], writes=[b_w3], dma="dsp")
            p.emit("act", lambda e, k=k: e.activation(op1[:, k * 1024:(k + 1) * 1024], wstg3.ap(), AF.Copy),
                   reads=[b_w3], writes=[b_op1])
        for k in range(8):
            p.emit("sp", lambda e, k=k: e.dma_start(wstg3.ap(), d["l1a_op2"].ap()[k * 128:(k + 1) * 128, :]),
                   reads=[b_w3], writes=[b_w3], dma="dsp")
            p.emit("act", lambda e, k=k: e.activation(op2[:, k * 1024:(k + 1) * 1024], wstg3.ap(), AF.Copy),
                   reads=[b_w3], writes=[b_op2])

        src_ap = xaps[NL % 2]
        for t in range(NCH):
            t0 = t * T
            p.emit("sp", lambda e, t0=t0: e.dma_start(xa.ap(), src_ap[:, :, t0:t0 + T]),
                   reads=[b_xa], writes=[b_xa], dma="dsp")
            p.emit("act", lambda e: e.activation(xnb.ap(), xa.ap(), AF.Copy),
                   reads=[b_xa], writes=[b_xnb])
            for m in range(8):
                pj = next(prot)

                def mm1(e, m=m, pj=pj):
                    last = None
                    for k in range(8):
                        last = e.matmul(ps[pj][:, 0:T], op1[:, k * 1024 + m * 128:k * 1024 + (m + 1) * 128],
                                        xnb.ap()[:, k, 0:T], start=(k == 0), stop=(k == 7))
                    return last
                p.emit("pe", mm1, reads=[b_op1, b_xnb], writes=[b_ps[pj]])
                p.emit("act", lambda e, m=m, pj=pj: e.activation(
                    cbf.ap()[:, m, 0:T], ps[pj][:, 0:T], AF.Gelu, bias=opb[:, m:m + 1]),
                    reads=[b_ps[pj], b_opb], writes=[b_cb])
            p.emit("sp", lambda e, t0=t0: e.dma_start(hst.ap(), hs_ap[:, :, t0:t0 + T]),
                   reads=[b_hst], writes=[b_hst], dma="dsp")
            for m in range(8):
                pj = next(prot)

                def mm2(e, m=m, pj=pj):
                    last = None
                    for k in range(8):
                        last = e.matmul(ps[pj][:, 0:T], op2[:, k * 1024 + m * 128:k * 1024 + (m + 1) * 128],
                                        cbf.ap()[:, k, 0:T], start=(k == 0), stop=(k == 7))
                    return last
                p.emit("pe", mm2, reads=[b_op2, b_cb], writes=[b_ps[pj]])
                p.emit("act", lambda e, m=m, pj=pj: e.activation(
                    obuf.ap()[:, m, 0:T], ps[pj][:, 0:T], AF.Identity, bias=opb[:, 8 + m:9 + m]),
                    reads=[b_ps[pj], b_opb], writes=[b_ob])
            p.emit("dve", lambda e: e.tensor_tensor(obuf.ap(), obuf.ap(), hst.ap(), ALU.add),
                   reads=[b_ob, b_hst], writes=[b_ob])
            kit.stats(cbf.ap(), sqs.ap(), b_cb, obuf.ap(), b_ob, 8, 1024)
            kit.bcast()
            p.emit("dve", lambda e: e.tensor_tensor(
                obuf.ap(), obuf.ap(), bc_mid(kit.psb0.ap()[:, 0:T], 8, T), ALU.add),
                reads=[b_ob, kit.b_psb0], writes=[b_ob])
            p.emit("dve", lambda e: e.tensor_tensor(
                obuf.ap(), obuf.ap(), bc_mid(kit.psb1.ap()[:, 0:T], 8, T), ALU.mult),
                reads=[b_ob, kit.b_psb1], writes=[b_ob])
            p.emit("sp", lambda e, t0=t0: e.dma_start(hst.ap(), xin_ap[:, :, t0:t0 + T]),
                   reads=[b_hst], writes=[b_hst], dma="dsp")
            p.emit("dve", lambda e: e.tensor_tensor(obuf.ap(), obuf.ap(), hst.ap(), ALU.add),
                   reads=[b_ob, b_hst], writes=[b_ob])
            p.emit("pool", lambda e, t0=t0: e.dma_start(xout_ap[:, :, t0:t0 + T], obuf.ap()),
                   reads=[b_ob], dma="dpool")

        p.barrier()
        p.run_block()
    es0.close()



def stage_l1f(p, d, S, T=512, NLY=6):
    nc = p.nc
    NCH = S // T
    HDS = 16

    es0 = ExitStack()
    a0 = lambda n_, s_, t_: es0.enter_context(nc.sbuf_tensor(n_, s_, t_))
    apsum = lambda n_, s_: es0.enter_context(nc.psum_tensor(n_, s_, F32))
    kit = LNKit(p, a0, apsum, T, "f1")
    ps = [apsum(f"f1_ps{i}", [128, 512]) for i in range(3)]
    b_ps = [Buf() for _ in range(3)]
    pst = es0.enter_context(nc.psum_tensor("f1_pst", [128, 512], BF16))
    b_pst = Buf()
    import itertools
    prot = itertools.cycle(range(3))

    xt = a0("f1_xt", [128, 8, T + 1], F32)
    cbf = a0("f1_cbf", [128, 8, T], BF16)
    scr = a0("f1_scr", [128, 8, T], F32)
    b_xt, b_cb, b_sq, b_scr = (Buf() for _ in range(4))

    xaps = [d["xa_a"].ap().rearrange("(c p) s -> p c s", p=128),
            d["xa_b"].ap().rearrange("(c p) s -> p c s", p=128)]
    xin_ap = d["x_in"].ap().rearrange("(c p) s -> p c s", p=128)
    xout_ap = d["x_out"].ap().rearrange("(c p) s -> p c s", p=128)

    def load_tile(dram_ap, t0, dst, b_dst, b_also=()):
        if t0 == 0:
            p.emit("dve", lambda e: e.memset(dst.ap()[:, :, 0:1], 0.0),
                   reads=[], writes=[b_dst])
            p.emit("sp", lambda e: e.dma_start(dst.ap()[:, :, 1:T + 1], dram_ap[:, :, 0:T]),
                   reads=[b_dst], writes=[b_dst], dma="dsp")
        else:
            p.emit("sp", lambda e: e.dma_start(dst.ap(), dram_ap[:, :, t0 - 1:t0 + T]),
                   reads=[b_dst], writes=[b_dst], dma="dsp")

    es_pre = ExitStack()
    sqb = es_pre.enter_context(nc.sbuf_tensor("f1_sqpre", [128, 8, T], BF16))
    for t in range(NCH):
        t0 = t * T
        p.emit("sp", lambda e, t0=t0: e.dma_start(xt.ap()[:, :, 1:T + 1], xin_ap[:, :, t0:t0 + T]),
               reads=[b_xt], writes=[b_xt], dma="dsp")
        kit.stats(cbf.ap(), sqb.ap(), b_cb, xt.ap()[:, :, 1:T + 1], b_xt, 8, 1024)
        kit.bcast()
        p.emit("dve", lambda e: e.tensor_tensor(
            scr.ap(), xt.ap()[:, :, 1:T + 1], bc_mid(kit.psb0.ap()[:, 0:T], 8, T), ALU.add),
            reads=[b_xt, kit.b_psb0], writes=[b_scr])
        p.emit("dve", lambda e: e.tensor_tensor(
            scr.ap(), scr.ap(), bc_mid(kit.psb1.ap()[:, 0:T], 8, T), ALU.mult),
            reads=[b_scr, kit.b_psb1], writes=[b_scr])
        p.emit("pool", lambda e, t0=t0: e.dma_start(xaps[0][:, :, t0:t0 + T], scr.ap()),
               reads=[b_scr], dma="dpool")

    p.barrier()
    p.run_block()
    es_pre.close()

    for ly in range(NLY):
        src_ap = xaps[0]
        mid_ap = xaps[1]
        CQ = ly * 26

        with ExitStack() as es1:
            a1 = lambda n_, s_, t_: es1.enter_context(nc.sbuf_tensor(f"{n_}_{ly}", s_, t_))
            wqkv = a1("f1_wqkv", [128, 8 * 3072], BF16)
            wo = a1("f1_wo", [128, 8 * 1024], BF16)
            colsA = a1("f1_colsA", [128, 156], F32)
            identb = a1("f1_identb", [128, 128], BF16)
            maskb = a1("f1_maskb", [128, 3, 384], BF16)
            sqb = a1("f1_sqb", [128, 8, T], BF16)
            qb = a1("f1_qb", [128, 8, T], BF16)
            kb = a1("f1_kb", [128, 8, T + 256], BF16)
            qsw = a1("f1_qsw", [128, 8, T], BF16)
            vtm = a1("f1_vtm", [128, 6, 1024], BF16)
            rtb = a1("f1_rtb", [128, 4, T], BF16)
            nrm = a1("f1_nrm", [1, 4 * T], F32)
            rqb = a1("f1_rqb", [128, 8, T], BF16)
            rkb = a1("f1_rkb", [128, 8, T], BF16)
            att = a1("f1_att", [128, 8, T], BF16)
            Ebuf = [a1(f"f1_E{i}", [128, 384], BF16) for i in range(2)]
            ETb = [a1(f"f1_ET{i}", [128, 384], BF16) for i in range(2)]
            esum = a1("f1_esum", [128, 2], F32)
            rtf = scr
            b_rtf = b_scr
            (b_wq, b_wo, b_colsA, b_id, b_mask, b_qb, b_kb, b_qsw, b_vtm,
             b_rtb, b_nrm, b_rqb, b_rkb, b_att, b_esum) = (Buf() for _ in range(15))
            b_E = [Buf(), Buf()]
            b_ET = [Buf(), Buf()]

            wst = xt.ap().rearrange("p a b -> p (a b)")[:, 0:3072]
            for k in range(8):
                p.emit("sp", lambda e, k=k, ly=ly: e.dma_start(
                    wst, d["l1f_wqkv"].ap()[ly, k * 128:(k + 1) * 128, :]),
                    reads=[b_xt], writes=[b_xt], dma="dsp")
                p.emit("act", lambda e, k=k: e.activation(
                    wqkv[:, k * 3072:(k + 1) * 3072], wst, AF.Copy),
                    reads=[b_xt], writes=[b_wq])
            wst2 = xt.ap().rearrange("p a b -> p (a b)")[:, 0:1024]
            for k in range(8):
                p.emit("sp", lambda e, k=k, ly=ly: e.dma_start(
                    wst2, d["l1f_wo"].ap()[ly, k * 128:(k + 1) * 128, :]),
                    reads=[b_xt], writes=[b_xt], dma="dsp")
                p.emit("act", lambda e, k=k: e.activation(
                    wo[:, k * 1024:(k + 1) * 1024], wst2, AF.Copy),
                    reads=[b_xt], writes=[b_wo])
            p.emit("sp", lambda e: e.dma_start(colsA[:], d["l1f_cols"].ap()), writes=[b_colsA], dma="dsp")
            wst3 = xt.ap().rearrange("p a b -> p (a b)")[:, 0:3 * 384].rearrange("p (a b) -> p a b", a=3)
            p.emit("sp", lambda e: e.dma_start(wst3, d["attnmask"].ap().transpose([1, 0, 2])),
                   reads=[b_xt], writes=[b_xt], dma="dsp")
            p.emit("act", lambda e: e.activation(maskb.ap(), wst3, AF.Copy),
                   reads=[b_xt], writes=[b_mask])
            wst4 = xt.ap().rearrange("p a b -> p (a b)")[:, 0:128]
            p.emit("sp", lambda e: e.dma_start(wst4, d["diagmask"].ap()),
                   reads=[b_xt], writes=[b_xt], dma="dsp")
            p.emit("act", lambda e: e.activation(identb[:], wst4, AF.Copy),
                   reads=[b_xt], writes=[b_id])

            for t in range(NCH):
                t0 = t * T
                if t == 0:
                    p.emit("dve", lambda e: e.memset(kb.ap()[:, :, 0:256], 0.0), writes=[b_kb])
                    p.emit("dve", lambda e: e.memset(vtm.ap()[:, 0:2, :], 0.0), writes=[b_vtm])
                load_tile(src_ap, t0, xt, b_xt)
                p.emit("act", lambda e: e.activation(cbf.ap()[:, 0:4, 0:T], xt.ap()[:, 0:4, 1:T + 1], AF.Copy),
                       reads=[b_xt], writes=[b_cb])
                p.emit("act", lambda e: e.activation(cbf.ap()[:, 4:8, 0:T], xt.ap()[:, 4:8, 0:T], AF.Copy),
                       reads=[b_xt], writes=[b_cb])

                p.emit("sp", lambda e, t0=t0: e.dma_start(
                    rtf.ap()[:, 0:2, :], d["rot_q"].ap()[:, :, t0:t0 + T].transpose([1, 0, 2])),
                    reads=[b_rtf], writes=[b_rtf], dma="dsp")
                p.emit("sp", lambda e, t0=t0: e.dma_start(
                    rtf.ap()[:, 2:4, :], d["rot_k"].ap()[:, :, t0:t0 + T].transpose([1, 0, 2])),
                    reads=[b_rtf], writes=[b_rtf], dma="dsp")
                p.emit("act", lambda e: e.activation(rtb.ap(), rtf.ap()[:, 0:4, :], AF.Copy),
                       reads=[b_rtf], writes=[b_rtb])

                for m in range(16):
                    pj = next(prot)

                    def qk_mm(e, m=m, pj=pj):
                        last = None
                        for k in range(8):
                            last = e.matmul(ps[pj][:, 0:T],
                                            wqkv[:, k * 3072 + m * 128:k * 3072 + (m + 1) * 128],
                                            cbf.ap()[:, k, 0:T], start=(k == 0), stop=(k == 7))
                        return last
                    p.emit("pe", qk_mm, reads=[b_wq, b_cb], writes=[b_ps[pj]])
                    if m < 8:
                        p.emit("act", lambda e, m=m, pj=pj: e.activation(
                            qb.ap()[:, m, 0:T], ps[pj][:, 0:T], AF.Copy),
                            reads=[b_ps[pj]], writes=[b_qb])
                    else:
                        p.emit("act", lambda e, m=m, pj=pj: e.activation(
                            kb.ap()[:, m - 8, 256:T + 256], ps[pj][:, 0:T], AF.Copy),
                            reads=[b_ps[pj]], writes=[b_kb])

                for ts_ in range(4):
                    for half in range(2):
                        pj = next(prot)

                        def v_mm(e, ts_=ts_, half=half, pj=pj):
                            last = None
                            for k in range(8):
                                last = e.matmul(
                                    ps[pj][:, 0:512],
                                    cbf.ap()[:, k, ts_ * 128:(ts_ + 1) * 128],
                                    wqkv[:, k * 3072 + 2048 + half * 512: k * 3072 + 2048 + (half + 1) * 512],
                                    start=(k == 0), stop=(k == 7))
                            return last
                        p.emit("pe", v_mm, reads=[b_wq, b_cb], writes=[b_ps[pj]])
                        p.emit("act", lambda e, ts_=ts_, half=half, pj=pj: e.activation(
                            vtm.ap()[:, 2 + ts_, half * 512:(half + 1) * 512], ps[pj][:, 0:512], AF.Copy),
                            reads=[b_ps[pj]], writes=[b_vtm])

                for side, srcb, b_srcb, dstb, b_dstb, koff in (
                        (0, qb, b_qb, rqb, b_rqb, 0), (1, kb, b_kb, rkb, b_rkb, 256)):
                    p.emit("act", lambda e, srcb=srcb, koff=koff: e.activation(
                        sqb.ap(), srcb.ap()[:, :, koff:koff + T] if koff else srcb.ap()[:, 0:8, 0:T],
                        AF.Square),
                        reads=[b_srcb], writes=[b_sq])
                    for qq in range(4):
                        for hh in range(4):
                            h = qq * 4 + hh
                            pj = next(prot)
                            c, sub = h // 2, h % 2
                            p.emit("pe", lambda e, c=c, sub=sub, pj=pj: e.matmul(
                                ps[pj][0:1, 0:T], kit.ones_bf[64 * sub:64 * (sub + 1), :],
                                sqb.ap()[64 * sub:64 * (sub + 1), c, 0:T],
                                start=True, stop=True),
                                reads=[b_sq, kit.b_onebf], writes=[b_ps[pj]])
                            p.emit("act", lambda e, hh=hh, pj=pj: e.activation(
                                nrm[0:1, hh * T:(hh + 1) * T], ps[pj][0:1, 0:T], AF.Copy),
                                reads=[b_ps[pj]], writes=[b_nrm])
                        p.emit("act", lambda e: e.activation(nrm[0:1, :], nrm[0:1, :], AF.Sqrt),
                               reads=[b_nrm], writes=[b_nrm])
                        p.emit("dve", lambda e: e.tensor_scalar(nrm[0:1, :], nrm[0:1, :], 1e-12, None, ALU.max),
                               reads=[b_nrm], writes=[b_nrm])
                        p.emit("dve", lambda e: e.reciprocal(nrm[0:1, :], nrm[0:1, :]),
                               reads=[b_nrm], writes=[b_nrm])
                        for hh in range(4):
                            h = qq * 4 + hh
                            pj = next(prot)
                            c, sub = h // 2, h % 2
                            p.emit("pe", lambda e, hh=hh, pj=pj, sub=sub: e.matmul(
                                ps[pj][64 * sub:64 * (sub + 1), 0:T], kit.ones_f[:, 0:64],
                                nrm[0:1, hh * T:(hh + 1) * T], start=True, stop=True),
                                reads=[b_nrm, kit.b_onef], writes=[b_ps[pj]])
                            p.emit("act", lambda e, c=c, sub=sub, pj=pj, dstb=dstb: e.activation(
                                dstb.ap()[64 * sub:64 * (sub + 1), c, 0:T],
                                ps[pj][64 * sub:64 * (sub + 1), 0:T], AF.Copy),
                                reads=[b_ps[pj]], writes=[b_dstb])

                for side, srcb, b_srcb, swb, b_swb, rb, b_rb, c0, koff in (
                        (0, qb, b_qb, qsw, b_qsw, rqb, b_rqb, 0, 0),
                        (1, kb, b_kb, qsw, b_qsw, rkb, b_rkb, 2, 256)):
                    sv = srcb.ap()[:, 0:8, koff:koff + T] if koff else srcb.ap()[:, 0:8, 0:T]
                    if side == 0:
                        p.emit("dve", lambda e, sv=sv: e.tensor_scalar(
                            sv, sv, colsA[:, CQ:CQ + 1], 8.0, ALU.mult, ALU.mult),
                            reads=[b_srcb, b_colsA], writes=[b_srcb])
                    else:
                        p.emit("dve", lambda e, sv=sv: e.tensor_scalar(
                            sv, sv, colsA[:, CQ + 1:CQ + 2], None, ALU.mult),
                            reads=[b_srcb, b_colsA], writes=[b_srcb])
                    for blk in range(4):
                        lo = blk * 32
                        sl = lo ^ 32
                        p.emit("sp", lambda e, lo=lo, sl=sl, swb=swb, srcb=srcb, koff=koff: e.dma_start(
                            swb.ap()[lo:lo + 32, :, :],
                            srcb.ap()[sl:sl + 32, 0:8, koff:koff + T]),
                            reads=[b_srcb, b_swb], writes=[b_swb], dma="dsp")
                    p.emit("dve", lambda e, sv=sv, c0=c0: e.tensor_tensor(
                        sv, sv, bc_mid(rtb.ap()[:, c0, :], 8, T), ALU.mult),
                        reads=[b_srcb, b_rtb], writes=[b_srcb])
                    p.emit("dve", lambda e, swb=swb, c0=c0: e.tensor_tensor(
                        swb.ap(), swb.ap(), bc_mid(rtb.ap()[:, c0 + 1, :], 8, T), ALU.mult),
                        reads=[b_swb, b_rtb], writes=[b_swb])
                    p.emit("dve", lambda e, sv=sv, swb=swb: e.tensor_tensor(
                        sv, sv, swb.ap(), ALU.add),
                        reads=[b_srcb, b_swb], writes=[b_srcb])
                    p.emit("dve", lambda e, sv=sv, rb=rb: e.tensor_tensor(
                        sv, sv, rb.ap(), ALU.mult),
                        reads=[b_srcb, b_rb], writes=[b_srcb])

                for qs_ in range(4):
                    for h in range(HDS):
                        c, sub = h // 2, h % 2
                        par = (qs_ * HDS + h) % 2
                        pj = next(prot)
                        mi = (1 + qs_) if (t == 0 and qs_ < 2) else 0

                        def unit_mm(e, c=c, sub=sub, qs_=qs_, pj=pj, mi=mi):
                            last = e.matmul(ps[pj][:, 0:384], identb[:], maskb.ap()[:, mi, :],
                                            start=True, stop=False)
                            last = e.matmul(ps[pj][:, 0:384],
                                            qb.ap()[64 * sub:64 * (sub + 1), c, qs_ * 128:(qs_ + 1) * 128],
                                            kb.ap()[64 * sub:64 * (sub + 1), c, qs_ * 128:qs_ * 128 + 384],
                                            start=False, stop=True)
                            return last
                        p.emit("pe", unit_mm, reads=[b_id, b_mask, b_qb, b_kb], writes=[b_ps[pj]])
                        p.emit("dve", lambda e, pj=pj, par=par: e.tensor_reduce(
                            esum[:, par:par + 1], ps[pj][:, 0:384], AX.X, ALU.max, negate=True),
                            reads=[b_ps[pj]], writes=[b_esum])
                        p.emit("act", lambda e, pj=pj, par=par: e.activation(
                            Ebuf[par].ap(), ps[pj][:, 0:384], AF.Exp, bias=esum[:, par:par + 1]),
                            reads=[b_ps[pj], b_esum], writes=[b_E[par]])
                        p.emit("dve", lambda e, par=par: e.tensor_reduce(
                            esum[:, par:par + 1], Ebuf[par].ap(), AX.X, ALU.add),
                            reads=[b_E[par]], writes=[b_esum])
                        p.emit("dve", lambda e, par=par: e.reciprocal(
                            esum[:, par:par + 1], esum[:, par:par + 1]),
                            reads=[b_esum], writes=[b_esum])
                        p.emit("dve", lambda e, par=par: e.tensor_scalar(
                            Ebuf[par].ap(), Ebuf[par].ap(), esum[:, par:par + 1], None, ALU.mult),
                            reads=[b_E[par], b_esum], writes=[b_E[par]])
                        def tr_mm(e, par=par):
                            last = None
                            for kc in range(3):
                                last = e.transpose(pst[:, kc * 128:(kc + 1) * 128],
                                                   Ebuf[par].ap()[:, kc * 128:(kc + 1) * 128],
                                                   identb[:])
                            return last
                        p.emit("pe", tr_mm, reads=[b_E[par], b_id], writes=[b_pst])
                        p.emit("act", lambda e, par=par: e.activation(
                            ETb[par].ap(), pst[:, 0:384], AF.Copy),
                            reads=[b_pst], writes=[b_ET[par]])
                        pj3 = next(prot)

                        def av_mm(e, h=h, qs_=qs_, par=par, pj3=pj3):
                            last = None
                            for kc in range(3):
                                last = e.matmul(ps[pj3][0:64, 0:128],
                                                vtm.ap()[:, qs_ + kc, h * 64:(h + 1) * 64],
                                                ETb[par].ap()[:, kc * 128:(kc + 1) * 128],
                                                start=(kc == 0), stop=(kc == 2))
                            return last
                        p.emit("pe", av_mm, reads=[b_vtm, b_ET[par]], writes=[b_ps[pj3]])
                        p.emit("act", lambda e, c=c, sub=sub, qs_=qs_, pj3=pj3: e.activation(
                            att.ap()[64 * sub:64 * (sub + 1), c, qs_ * 128:(qs_ + 1) * 128],
                            ps[pj3][0:64, 0:128], AF.Copy),
                            reads=[b_ps[pj3]], writes=[b_att])

                for m in range(8):
                    pj = next(prot)

                    def wo_mm(e, m=m, pj=pj):
                        last = None
                        for k in range(8):
                            last = e.matmul(ps[pj][:, 0:T],
                                            wo[:, k * 1024 + m * 128:k * 1024 + (m + 1) * 128],
                                            att.ap()[:, k, 0:T], start=(k == 0), stop=(k == 7))
                        return last
                    p.emit("pe", wo_mm, reads=[b_wo, b_att], writes=[b_ps[pj]])
                    p.emit("dve", lambda e, m=m, pj=pj: e.tensor_tensor(
                        scr.ap()[:, m, 0:T], ps[pj][:, 0:T], xt.ap()[:, m, 1:T + 1], ALU.add),
                        reads=[b_ps[pj], b_xt], writes=[b_scr])
                p.emit("pool", lambda e, t0=t0: e.dma_start(mid_ap[:, :, t0:t0 + T], scr.ap()),
                       reads=[b_scr], dma="dpool")

                p.emit("dve", lambda e: e.tensor_copy(kb.ap()[:, :, 0:256], kb.ap()[:, :, T:T + 256]),
                       reads=[b_kb], writes=[b_kb])
                p.emit("dve", lambda e: e.tensor_copy(vtm.ap()[:, 0:2, :], vtm.ap()[:, 4:6, :]),
                       reads=[b_vtm], writes=[b_vtm])
                if t == 0:
                    pass

        p.barrier()
        p.run_block()

        with ExitStack() as es2:
            a2_ = lambda n_, s_, t_: es2.enter_context(nc.sbuf_tensor(f"{n_}_{ly}", s_, t_))
            w1b = a2_("f1_w1b", [128, 8 * 5632], BF16)
            w2b = a2_("f1_w2b", [128, 22 * 1024], BF16)
            colsF = a2_("f1_colsF", [128, 156], F32)
            b1c = a2_("f1_b1c", [128, 264], F32)
            hb = a2_("f1_hb", [128, 22, T], BF16)
            geb = a2_("f1_geb", [128, T], BF16)
            (b_w1, b_w2, b_colsF, b_b1c, b_hb, b_ge) = (Buf() for _ in range(6))
            xlb, b_xlb = cbf, b_cb

            p.emit("sp", lambda e: e.dma_start(colsF[:], d["l1f_cols"].ap()), writes=[b_colsF], dma="dsp")
            p.emit("sp", lambda e: e.dma_start(b1c[:], d["l1f_b1"].ap()), writes=[b_b1c], dma="dsp")
            wst = xt.ap().rearrange("p a b -> p (a b)")[:, 0:2816]
            for k in range(8):
                for hf in range(2):
                    p.emit("sp", lambda e, k=k, hf=hf, ly=ly: e.dma_start(
                        wst, d["l1f_w1p"].ap()[ly, k * 128:(k + 1) * 128, hf * 2816:(hf + 1) * 2816]),
                        reads=[b_xt], writes=[b_xt], dma="dsp")
                    p.emit("act", lambda e, k=k, hf=hf: e.activation(
                        w1b[:, k * 5632 + hf * 2816: k * 5632 + (hf + 1) * 2816], wst, AF.Copy),
                        reads=[b_xt], writes=[b_w1])
            wst2 = xt.ap().rearrange("p a b -> p (a b)")[:, 0:1024]
            for k in range(22):
                p.emit("sp", lambda e, k=k, ly=ly: e.dma_start(
                    wst2, d["l1f_w2p"].ap()[ly, k * 128:(k + 1) * 128, :]),
                    reads=[b_xt], writes=[b_xt], dma="dsp")
                p.emit("act", lambda e, k=k: e.activation(
                    w2b[:, k * 1024:(k + 1) * 1024], wst2, AF.Copy),
                    reads=[b_xt], writes=[b_w2])

            for t in range(NCH):
                t0 = t * T
                load_tile(mid_ap, t0, xt, b_xt)
                p.emit("dve", lambda e: e.tensor_copy(scr.ap()[:, 0:4, :], xt.ap()[:, 0:4, 1:T + 1]),
                       reads=[b_xt], writes=[b_scr])
                p.emit("dve", lambda e: e.tensor_copy(scr.ap()[:, 4:8, :], xt.ap()[:, 4:8, 0:T]),
                       reads=[b_xt], writes=[b_scr])
                kit.stats(cbf.ap(), hb.ap()[:, 0:8, 0:T], b_hb, scr.ap(), b_scr, 8, 1024)
                kit.bcast()
                p.emit("dve", lambda e: e.tensor_tensor(
                    scr.ap(), scr.ap(), bc_mid(kit.psb0.ap()[:, 0:T], 8, T), ALU.add),
                    reads=[b_scr, kit.b_psb0], writes=[b_scr])
                p.emit("dve", lambda e: e.tensor_tensor(
                    scr.ap(), scr.ap(), bc_mid(kit.psb1.ap()[:, 0:T], 8, T), ALU.mult),
                    reads=[b_scr, kit.b_psb1], writes=[b_scr])
                for c in range(8):
                    p.emit("dve", lambda e, c=c, CQ=CQ: e.tensor_scalar(
                        xlb.ap()[:, c, 0:T], scr.ap()[:, c, 0:T],
                        colsF[:, CQ + 2 + c:CQ + 3 + c], colsF[:, CQ + 10 + c:CQ + 11 + c],
                        ALU.mult, ALU.add),
                        reads=[b_scr, b_colsF], writes=[b_xlb])

                for j in range(22):
                    pj = next(prot)

                    def a_mm(e, j=j, pj=pj):
                        last = None
                        for k in range(8):
                            last = e.matmul(ps[pj][:, 0:T],
                                            w1b[:, k * 5632 + (2 * j) * 128: k * 5632 + (2 * j + 1) * 128],
                                            xlb.ap()[:, k, 0:T], start=(k == 0), stop=(k == 7))
                        return last
                    p.emit("pe", a_mm, reads=[b_w1, b_xlb], writes=[b_ps[pj]])
                    p.emit("act", lambda e, j=j, pj=pj, ly=ly: e.activation(
                        hb.ap()[:, j, 0:T], ps[pj][:, 0:T], AF.Identity,
                        bias=b1c[:, ly * 44 + 2 * j: ly * 44 + 2 * j + 1]),
                        reads=[b_ps[pj], b_b1c], writes=[b_hb])
                    pj2 = next(prot)

                    def g_mm(e, j=j, pj2=pj2):
                        last = None
                        for k in range(8):
                            last = e.matmul(ps[pj2][:, 0:T],
                                            w1b[:, k * 5632 + (2 * j + 1) * 128: k * 5632 + (2 * j + 2) * 128],
                                            xlb.ap()[:, k, 0:T], start=(k == 0), stop=(k == 7))
                        return last
                    p.emit("pe", g_mm, reads=[b_w1, b_xlb], writes=[b_ps[pj2]])
                    p.emit("act", lambda e, j=j, pj2=pj2, ly=ly: e.activation(
                        geb.ap(), ps[pj2][:, 0:T], AF.Gelu,
                        bias=b1c[:, ly * 44 + 2 * j + 1: ly * 44 + 2 * j + 2]),
                        reads=[b_ps[pj2], b_b1c], writes=[b_ge])
                    p.emit("dve", lambda e, j=j: e.tensor_tensor(
                        hb.ap()[:, j, 0:T], hb.ap()[:, j, 0:T], geb.ap(), ALU.mult),
                        reads=[b_hb, b_ge], writes=[b_hb])

                for m in range(8):
                    pj = next(prot)

                    def w2_mm(e, m=m, pj=pj):
                        last = None
                        for k in range(22):
                            last = e.matmul(ps[pj][:, 0:T],
                                            w2b[:, k * 1024 + m * 128:k * 1024 + (m + 1) * 128],
                                            hb.ap()[:, k, 0:T], start=(k == 0), stop=(k == 21))
                        return last
                    p.emit("pe", w2_mm, reads=[b_w2, b_hb], writes=[b_ps[pj]])
                    p.emit("act", lambda e, m=m, pj=pj, CQ=CQ: e.activation(
                        scr.ap()[:, m, 0:T], ps[pj][:, 0:T], AF.Identity,
                        bias=colsF[:, CQ + 18 + m:CQ + 19 + m]),
                        reads=[b_ps[pj], b_colsF], writes=[b_scr])
                p.emit("dve", lambda e: e.tensor_tensor(
                    scr.ap(), scr.ap(), xt.ap()[:, :, 1:T + 1], ALU.add),
                    reads=[b_scr, b_xt], writes=[b_scr])
                p.emit("pool", lambda e, t0=t0: e.dma_start(xaps[0][:, :, t0:t0 + T], scr.ap()),
                       reads=[b_scr], dma="dpool")

        p.barrier()
        p.run_block()

    es_fin = ExitStack()
    sqb = es_fin.enter_context(nc.sbuf_tensor("f1_sqfin", [128, 8, T], BF16))
    for t in range(NCH):
        t0 = t * T
        p.emit("sp", lambda e, t0=t0: e.dma_start(xt.ap()[:, :, 1:T + 1], xaps[0][:, :, t0:t0 + T]),
               reads=[b_xt], writes=[b_xt], dma="dsp")
        kit.stats(cbf.ap(), sqb.ap(), b_cb, xt.ap()[:, :, 1:T + 1], b_xt, 8, 1024)
        kit.bcast()
        p.emit("dve", lambda e: e.tensor_tensor(
            scr.ap(), xt.ap()[:, :, 1:T + 1], bc_mid(kit.psb0.ap()[:, 0:T], 8, T), ALU.add),
            reads=[b_xt, kit.b_psb0], writes=[b_scr])
        p.emit("dve", lambda e: e.tensor_tensor(
            scr.ap(), scr.ap(), bc_mid(kit.psb1.ap()[:, 0:T], 8, T), ALU.mult),
            reads=[b_scr, kit.b_psb1], writes=[b_scr])
        p.emit("sp", lambda e, t0=t0: e.dma_start(xt.ap()[:, :, 1:T + 1], xin_ap[:, :, t0:t0 + T]),
               reads=[b_xt, b_scr], writes=[b_xt], dma="dsp")
        p.emit("dve", lambda e: e.tensor_tensor(
            scr.ap(), scr.ap(), xt.ap()[:, :, 1:T + 1], ALU.add),
            reads=[b_scr, b_xt], writes=[b_scr])
        p.emit("pool", lambda e, t0=t0: e.dma_start(xout_ap[:, :, t0:t0 + T], scr.ap()),
               reads=[b_scr], dma="dpool")

    p.barrier()
    p.run_block()
    es_fin.close()
    es0.close()



def colcat(*vecs):
    cols = []
    for v in vecs:
        v = np.asarray(v, dtype=np.float32).reshape(-1)
        c = v.shape[0] // 128
        cols.append(v.reshape(c, 128).T)
    return np.ascontiguousarray(np.concatenate(cols, axis=1))


def prep_l0f(inp):
    return {
        "l0f_w1": np.ascontiguousarray(inp["L0_ffn_W1"], dtype=np.float32),
        "l0f_w2": np.ascontiguousarray(inp["L0_ffn_W2"], dtype=np.float32),
        "l0f_bias": colcat(inp["L0_ffn_b1"], inp["L0_ffn_b2"]),
    }


def l0f_tensors(S):
    return {
        "x_in": ([1024, S], "ExternalInput"),
        "x_out": ([1024, S], "ExternalOutput"),
        "l0f_w1": ([1024, 4096], "ExternalInput"),
        "l0f_w2": ([4096, 1024], "ExternalInput"),
        "l0f_bias": ([128, 40], "ExternalInput"),
    }


def prep_l1f(inp, S=8192):
    d = {}
    d["l1f_wqkv"] = np.ascontiguousarray(np.asarray(inp["L1_ffn_Wqkv"], np.float32))
    d["l1f_wo"] = np.ascontiguousarray(np.asarray(inp["L1_ffn_Wo"], np.float32))
    W1 = np.asarray(inp["L1_ffn_W1"], np.float32)
    b1 = np.asarray(inp["L1_ffn_b1"], np.float32)
    A = np.zeros((6, 1024, 2816), np.float32)
    G = np.zeros((6, 1024, 2816), np.float32)
    A[:, :, :2730] = W1[:, :, :2730]
    G[:, :, :2730] = W1[:, :, 2730:]
    ba = np.zeros((6, 2816), np.float32)
    bg = np.zeros((6, 2816), np.float32)
    ba[:, :2730] = b1[:, :2730]
    bg[:, :2730] = b1[:, 2730:]
    W1p = np.zeros((6, 1024, 5632), np.float32)
    Ar = A.reshape(6, 1024, 22, 128)
    Gr = G.reshape(6, 1024, 22, 128)
    W1r = W1p.reshape(6, 1024, 44, 128)
    W1r[:, :, 0::2, :] = Ar
    W1r[:, :, 1::2, :] = Gr
    d["l1f_w1p"] = np.ascontiguousarray(W1p)
    b1p = np.zeros((6, 44, 128), np.float32)
    b1p[:, 0::2, :] = ba.reshape(6, 22, 128)
    b1p[:, 1::2, :] = bg.reshape(6, 22, 128)
    d["l1f_b1"] = np.ascontiguousarray(np.transpose(b1p, (2, 0, 1)).reshape(128, 264))
    W2 = np.asarray(inp["L1_ffn_W2"], np.float32)
    W2p = np.zeros((6, 2816, 1024), np.float32)
    W2p[:, :2730, :] = W2
    d["l1f_w2p"] = np.ascontiguousarray(W2p)
    sets = []
    for ly in range(6):
        qs = np.tile(np.asarray(inp["L1_ffn_q_scale"], np.float32)[ly], 2)
        ks = np.tile(np.asarray(inp["L1_ffn_k_scale"], np.float32)[ly], 2)
        sets.append(colcat(qs, ks, inp["L1_ffn_ln_g"][ly], inp["L1_ffn_ln_b"][ly], inp["L1_ffn_b2"][ly]))
    d["l1f_cols"] = np.ascontiguousarray(np.concatenate(sets, axis=1))
    n, DH, W = S, 64, 256
    half = np.arange(0, DH, 2, dtype=np.float32)
    inv_freq = 1.0 / 10000.0 ** (half / DH)
    tpos = np.arange(n, dtype=np.float32)
    freqs = tpos[:, None] * inv_freq[None, :]
    freqs = np.concatenate([freqs, freqs], -1)
    base_scale = (half + 0.4 * DH) / (1.4 * DH)
    power = (tpos - n // 2) / (W // 2)
    scale = base_scale[None, :] ** power[:, None]
    scale = np.concatenate([scale, scale], -1)
    cos = np.cos(freqs)
    sin = np.sin(freqs)
    sgn = np.concatenate([-np.ones(32, np.float32), np.ones(32, np.float32)])
    csq = (cos * scale).T
    ssq = (sin * scale * sgn[None, :]).T
    csk = (cos / scale).T
    ssk = (sin / scale * sgn[None, :]).T
    d["rot_q"] = np.ascontiguousarray(np.stack([np.concatenate([csq, csq], 0),
                                                np.concatenate([ssq, ssq], 0)]).astype(np.float32))
    d["rot_k"] = np.ascontiguousarray(np.stack([np.concatenate([csk, csk], 0),
                                                np.concatenate([ssk, ssk], 0)]).astype(np.float32))
    i = np.arange(128)[:, None]
    j = np.arange(384)[None, :]
    base = (j >= i) & (j <= i + 256)
    m0 = np.where(base, 0.0, -1e30).astype(np.float32)
    m1 = np.where(base & (j >= 256), 0.0, -1e30).astype(np.float32)
    m2 = np.where(base & (j >= 128), 0.0, -1e30).astype(np.float32)
    d["attnmask"] = np.stack([m0, m1, m2])
    d["diagmask"] = np.eye(128, dtype=np.float32)
    return d


def l1f_tensors(S):
    return {
        "x_in": ([1024, S], "ExternalInput"),
        "x_out": ([1024, S], "ExternalOutput"),
        "l1f_wqkv": ([6, 1024, 3072], "ExternalInput"),
        "l1f_wo": ([6, 1024, 1024], "ExternalInput"),
        "l1f_w1p": ([6, 1024, 5632], "ExternalInput"),
        "l1f_w2p": ([6, 2816, 1024], "ExternalInput"),
        "l1f_b1": ([128, 264], "ExternalInput"),
        "l1f_cols": ([128, 156], "ExternalInput"),
        "rot_q": ([2, 128, S], "ExternalInput"),
        "rot_k": ([2, 128, S], "ExternalInput"),
        "attnmask": ([3, 128, 384], "ExternalInput"),
        "diagmask": ([128, 128], "ExternalInput"),
        "xa_a": ([1024, S], "Internal"),
        "xa_b": ([1024, S], "Internal"),
    }


def prep_l1a(inp):
    d = {}
    d["l1a_inproj"] = np.ascontiguousarray(np.transpose(np.asarray(inp["L1_attn_in_proj_w"], np.float32), (0, 2, 1)))
    d["l1a_outproj"] = np.ascontiguousarray(np.transpose(np.asarray(inp["L1_attn_out_proj_w"], np.float32), (0, 2, 1)))
    d["l1a_xproj"] = np.ascontiguousarray(np.transpose(np.asarray(inp["L1_attn_x_proj_w"], np.float32), (0, 2, 1)))
    d["l1a_dtw"] = np.ascontiguousarray(np.asarray(inp["L1_attn_dt_proj_w"], np.float32)[:, :, 0])
    d["l1a_op1"] = np.ascontiguousarray(np.asarray(inp["L1_attn_op1_w"], np.float32).T)
    d["l1a_op2"] = np.ascontiguousarray(np.asarray(inp["L1_attn_op2_w"], np.float32).T)
    d["l1a_opb"] = colcat(inp["L1_attn_op1_b"], inp["L1_attn_op2_b"])
    d["l1a_innorm"] = colcat(inp["L1_attn_in_norm_g"], inp["L1_attn_in_norm_b"])
    colsets = [colcat(inp["L1_attn_in_norm_g"], inp["L1_attn_in_norm_b"])]
    for l in range(4):
        colsets.append(colcat(
            inp["L1_attn_ln_g"][l], inp["L1_attn_ln_b"][l],
            inp["L1_attn_dt_proj_b"][l], inp["L1_attn_D"][l],
            inp["L1_attn_mamba_ln_g"][l], inp["L1_attn_mamba_ln_b"][l]))
    d["l1a_cols"] = np.ascontiguousarray(np.concatenate(colsets, axis=1))
    cw = np.asarray(inp["L1_attn_conv_w"], np.float32)[:, :, :, ::-1]
    cw = cw.reshape(4, 2, 16, 128, 4)
    cw = np.transpose(cw, (3, 0, 1, 2, 4)).reshape(128, 512)
    d["l1a_convw"] = np.ascontiguousarray(cw)
    cb = np.asarray(inp["L1_attn_conv_b"], np.float32).reshape(4, 2, 16, 128)
    cb = np.transpose(cb, (3, 0, 1, 2)).reshape(128, 128)
    d["l1a_convb"] = np.ascontiguousarray(cb)
    d["diagmask"] = np.eye(128, dtype=np.float32)
    return d


def l1a_tensors(S):
    return {
        "x_in": ([1024, S], "ExternalInput"),
        "x_out": ([1024, S], "ExternalOutput"),
        "l1a_inproj": ([4, 1024, 4096], "ExternalInput"),
        "l1a_outproj": ([4, 2048, 1024], "ExternalInput"),
        "l1a_xproj": ([4, 2048, 33], "ExternalInput"),
        "l1a_dtw": ([4, 2048], "ExternalInput"),
        "l1a_op1": ([1024, 1024], "ExternalInput"),
        "l1a_op2": ([1024, 1024], "ExternalInput"),
        "l1a_opb": ([128, 16], "ExternalInput"),
        "l1a_innorm": ([128, 16], "ExternalInput"),
        "l1a_cols": ([128, 336], "ExternalInput"),
        "l1a_convw": ([128, 512], "ExternalInput"),
        "l1a_convb": ([128, 128], "ExternalInput"),
        "diagmask": ([128, 128], "ExternalInput"),
        "xa_a": ([1024, S], "Internal"),
        "xa_b": ([1024, S], "Internal"),
        "hs_save": ([1024, S], "Internal"),
        "wbf_ip": ([4, 32, 128, 1024], "Internal", BF16),
        "wbf_op": ([4, 8, 128, 2048], "Internal", BF16),
    }


def prep_l0a(inp):
    d = {}
    d["l0a_w_in"] = np.ascontiguousarray(inp["L0_attn_W_in"], dtype=np.float32)
    d["l0a_w_out"] = np.ascontiguousarray(inp["L0_attn_W_out"], dtype=np.float32)
    d["l0a_bias"] = colcat(inp["L0_attn_b_in"], inp["L0_attn_short_b"],
                           inp["L0_attn_inner_b"], inp["L0_attn_b_out"])
    sw = np.asarray(inp["L0_attn_short_w"], dtype=np.float32)[:, ::-1]
    iw = np.asarray(inp["L0_attn_inner_w"], dtype=np.float32)[:, ::-1]
    swc = np.transpose(sw.reshape(24, 128, 3), (1, 0, 2)).reshape(128, 72)
    iwc = np.transpose(iw.reshape(8, 128, 4), (1, 0, 2)).reshape(128, 32)
    d["l0a_convw"] = np.ascontiguousarray(np.concatenate([swc, iwc], axis=1))
    d["diagmask"] = np.eye(128, dtype=np.float32)
    return d



def build_stage_program(stage_fns, S, names_shapes):
    nc = bass.Bass("TRN2", target_bir_lowering=False)
    d = {}
    for name, spec in names_shapes.items():
        shape, kind = spec[0], spec[1]
        dt_ = spec[2] if len(spec) > 2 else F32
        d[name] = nc.dram_tensor(name, list(shape), dt_, kind=kind)
    p = Plan(nc)
    for fn in stage_fns:
        fn(p, d, S)
    return nc


def l0a_tensors(S):
    return {
        "x_in": ([1024, S], "ExternalInput"),
        "x_out": ([1024, S], "ExternalOutput"),
        "l0a_w_in": ([1024, 3072], "ExternalInput"),
        "l0a_w_out": ([1024, 1024], "ExternalInput"),
        "l0a_bias": ([128, 64], "ExternalInput"),
        "l0a_convw": ([128, 104], "ExternalInput"),
        "diagmask": ([128, 128], "ExternalInput"),
    }


def _run_stage(build_key, stage_fn, tensors, wd, x_in_percore):
    S = 8192
    nc = _NC_CACHE.get(build_key)
    if nc is None:
        nc = build_stage_program([stage_fn], S, tensors)
        _NC_CACHE[build_key] = nc
    in_maps = []
    for b in range(8):
        m = {"x_in": x_in_percore[b]}
        m.update(wd)
        in_maps.append(m)
    res = run_bass_kernel_spmd(nc, in_maps, core_ids=list(range(8)))
    return [r["x_out"] for r in res.results]


_NC_CACHE = {}


def kernel(**inputs):
    S = 8192
    u = np.asarray(inputs["L0_attn_u"], dtype=np.float32)
    xs = [np.ascontiguousarray(u[b].T) for b in range(8)]

    xs = _run_stage("l0a", stage_l0a, l0a_tensors(S), prep_l0a(inputs), xs)
    xs = _run_stage("l0f", stage_l0f, l0f_tensors(S), prep_l0f(inputs), xs)
    xs = _run_stage("l1a", stage_l1a, l1a_tensors(S), prep_l1a(inputs), xs)
    xs = _run_stage("l1f", stage_l1f, l1f_tensors(S), prep_l1f(inputs, S), xs)

    out = np.stack([x.T for x in xs], axis=0)
    return np.ascontiguousarray(out)
